# version 01
# baseline (speedup 1.0000x reference)
import time
import numpy as np
import concourse.bass as bass
import concourse.mybir as mybir
from concourse.bass_utils import run_bass_kernel_spmd
from concourse.expressions_rust import make_scalar_value, s_logical_and

F32 = mybir.dt.float32
BF16 = mybir.dt.bfloat16
I32 = mybir.dt.int32
AF = mybir.ActivationFunctionType
ALU = mybir.AluOpType
AX = mybir.AxisListType

H = 2048
DH = 2048
N = 16
FF = 5632
T = 4096
NKH = 16
NDT = 16
NMF = 44
TCH = 1024
POLL_CAP = 3000000
STAGE = 4


class Ctr:
    def __init__(self, sem, step=1):
        self.sem = sem
        self.n = 0
        self.step = step


class Ctx:
    pass



_WRAP_OPS = {
    "tensor_tensor", "tensor_add", "tensor_sub", "tensor_mul", "tensor_max",
    "tensor_scalar", "tensor_single_scalar", "tensor_scalar_add",
    "tensor_scalar_sub", "tensor_scalar_mul", "tensor_scalar_max",
    "tensor_scalar_min", "tensor_relu", "tensor_reduce", "reduce_max",
    "reduce_sum", "scalar_tensor_tensor", "tensor_tensor_scan", "tensor_copy",
    "memset", "memzero", "activation", "sign", "square", "sqrt", "mul",
    "copy", "add", "select", "copy_predicated", "reciprocal", "transpose",
}


class EW:

    def __init__(self, eng, ctr):
        self._e = eng
        self._c = ctr

    def __getattr__(self, name):
        fn = getattr(self._e, name)
        if name not in _WRAP_OPS:
            return fn

        def call(*a, **k):
            self._e.wait_ge(self._c.sem, self._c.n)
            r = fn(*a, **k)
            r.then_inc(self._c.sem, 1)
            self._c.n += 1
            return r
        return call


def build_program(stage=STAGE, dbg=(), pairless=False):
    nc = bass.Bass("TRN2", target_bir_lowering=False, debug=False)
    c = Ctx()
    c.nc = nc
    c.stage = stage
    c.pairless = pairless

    c.xT = nc.dram_tensor("xT", [H, T], F32, kind="ExternalInput")
    c.w_inT = nc.dram_tensor("w_inT", [2 * NDT, NKH, 128, 128], F32, kind="ExternalInput")
    c.convw = nc.dram_tensor("convw", [NDT, 128, 4], F32, kind="ExternalInput")
    c.convb = nc.dram_tensor("convb", [128, NDT], F32, kind="ExternalInput")
    c.bdt = nc.dram_tensor("bdt", [128, NDT], F32, kind="ExternalInput")
    c.dskip = nc.dram_tensor("dskip", [128, NDT], F32, kind="ExternalInput")
    c.alog = nc.dram_tensor("alog", [128, NDT * N], F32, kind="ExternalInput")
    c.w_xT = nc.dram_tensor("w_xT", [NDT, 128, 160], F32, kind="ExternalInput")
    c.w_dtT = nc.dram_tensor("w_dtT", [NDT, 128, 128], F32, kind="ExternalInput")
    c.dtln_row = nc.dram_tensor("dtln_row", [1, 128], F32, kind="ExternalInput")
    c.bc_sel = nc.dram_tensor("bc_sel", [2, 32], F32, kind="ExternalInput")
    c.grp_sel = nc.dram_tensor("grp_sel", [32, 2], F32, kind="ExternalInput")
    c.inln_row = nc.dram_tensor("inln_row", [1, H], F32, kind="ExternalInput")
    c.preln_row = nc.dram_tensor("preln_row", [1, H], F32, kind="ExternalInput")
    c.ident = nc.dram_tensor("ident", [128, 128], F32, kind="ExternalInput")
    c.w_outT = nc.dram_tensor("w_outT", [NKH, NDT, 128, 128], F32, kind="ExternalInput")
    c.w_gT = nc.dram_tensor("w_gT", [NMF, NKH, 128, 128], F32, kind="ExternalInput")
    c.w_uT = nc.dram_tensor("w_uT", [NMF, NKH, 128, 128], F32, kind="ExternalInput")
    c.w_dnT = nc.dram_tensor("w_dnT", [NKH, NMF, 128, 128], F32, kind="ExternalInput")
    c.w_tern = nc.dram_tensor("w_tern", [H, H], F32, kind="ExternalInput")
    c.ffnb = nc.dram_tensor("ffnb", [128, NKH], F32, kind="ExternalInput")
    c.role_h = nc.dram_tensor("role", [1, 1], I32, kind="ExternalInput")
    c.nonce_h = nc.dram_tensor("nonce", [1, 1], I32, kind="ExternalInput")

    c.out_fm = nc.dram_tensor("out_fm", [H, 2 * TCH], F32, kind="ExternalOutput")
    c.dbg = {}
    for name, shape, dt in dbg:
        c.dbg[name] = nc.dram_tensor("dbg_" + name, list(shape), dt, kind="ExternalOutput")

    c.u_spill = nc.dram_tensor("u_spill", [NDT, 128, T], BF16)
    c.sg_spill = nc.dram_tensor("sg_spill", [NDT, 128, T], BF16)
    c.xn_spill = nc.dram_tensor("xn_spill", [NKH, 128, T], BF16)
    c.stats_scr = nc.dram_tensor("stats_scr", [4, T], F32)
    c.bc_scr = nc.dram_tensor("bc_scr", [32, T], BF16)
    c.qT_cache = nc.dram_tensor("qT_cache", [NKH, NKH, 128, 128], BF16)
    c.mixer_mine = nc.dram_tensor("mixer_mine", [2, NKH, 128, TCH], F32)
    c.hidden_spill = nc.dram_tensor("hidden_spill", [2, NKH, 128, TCH], F32)
    c.v_spill = nc.dram_tensor("v_spill", [NMF, 128, TCH], BF16)
    c.g_spill = nc.dram_tensor("g_spill", [NKH, 128, TCH], BF16)

    c.sh_ssm = [nc.dram_tensor("sh_ssmE", [160, T], F32, addr_space="Shared"),
                nc.dram_tensor("sh_ssmO", [160, T], F32, addr_space="Shared")]
    c.sh_mix = [nc.dram_tensor("sh_mixE", [2, NKH, 128, TCH], F32, addr_space="Shared"),
                nc.dram_tensor("sh_mixO", [2, NKH, 128, TCH], F32, addr_space="Shared")]
    c.fl = [nc.dram_tensor("fl_E", [1, 16], I32, addr_space="Shared"),
            nc.dram_tensor("fl_O", [1, 16], I32, addr_space="Shared")]

    from contextlib import ExitStack
    with ExitStack() as ctx:
        en = ctx.enter_context

        c.ar1 = en(nc.sbuf_tensor("ar1", [128, 32768], BF16))
        c.ar2 = en(nc.sbuf_tensor("ar2", [128, 24576], BF16))
        c.af = en(nc.sbuf_tensor("af", [128, 5120], F32))
        c.ssm1 = en(nc.sbuf_tensor("ssm1", [128, T], F32))
        c.ssmA = en(nc.sbuf_tensor("ssmA", [128, T], F32))
        c.s_trn = en(nc.sbuf_tensor("s_trn", [128, T], BF16))
        c.s_cst = en(nc.sbuf_tensor("s_cst", [128, 64], F32))
        c.s_alogA = en(nc.sbuf_tensor("s_alogA", [128, 256], F32))
        c.s_convw = en(nc.sbuf_tensor("s_convw", [128, 64], F32))
        c.s_wdtT = en(nc.sbuf_tensor("s_wdtT", [128, NDT * 128], BF16))
        c.s_upre = en(nc.sbuf_tensor("s_upre", [128, 515], F32))
        c.s_stash = en(nc.sbuf_tensor("s_stash", [128, 48], F32))
        c.s_hstate = en(nc.sbuf_tensor("s_hstate", [128, 256], BF16))
        c.s_ident = en(nc.sbuf_tensor("s_ident", [128, 128], BF16))
        c.s_onesc = en(nc.sbuf_tensor("s_onesc", [128, 1], BF16))
        c.s_inln = en(nc.sbuf_tensor("s_inln", [1, H], BF16))
        c.s_preln = en(nc.sbuf_tensor("s_preln", [1, H], BF16))
        c.s_dtln = en(nc.sbuf_tensor("s_dtln", [1, 128], BF16))
        c.s_bcsel = en(nc.sbuf_tensor("s_bcsel", [2, 32], BF16))
        c.s_grpsel = en(nc.sbuf_tensor("s_grpsel", [32, 2], BF16))
        c.s_rowt = en(nc.sbuf_tensor("s_rowt", [2, 1024], BF16))
        c.s_rowf = en(nc.sbuf_tensor("s_rowf", [1, 2048], F32))
        c.s_row2 = en(nc.sbuf_tensor("s_row2", [2, 512], F32))
        c.s_bcn = en(nc.sbuf_tensor("s_bcn", [32, 512], BF16))
        c.s_sq2 = en(nc.sbuf_tensor("s_sq2", [32, 512], BF16))
        c.s_flags = en(nc.sbuf_tensor("s_flags", [1, 16], I32))
        c.s_flagrd = en(nc.sbuf_tensor("s_flagrd", [1, 1], I32))
        c.s_role = en(nc.sbuf_tensor("s_role", [1, 1], I32))
        c.s_nonce = en(nc.sbuf_tensor("s_nonce", [1, 1], I32))
        c.s_kc = en(nc.sbuf_tensor("s_kc", [128, 8], F32))
        nc.const_aps.register(F32, 1e-5, c.s_kc[:, 0:1])
        nc.const_aps.register(F32, 1e-6, c.s_kc[:, 1:2])
        nc.const_aps.register(F32, 1.0, c.s_kc[:, 2:3])

        c.p = [en(nc.psum_tensor(f"p{i}", [128, 512], F32)) for i in range(6)]
        c.pT = en(nc.psum_tensor("pT", [128, 128], BF16))
        c.pst = en(nc.psum_tensor("pst", [1, 512], F32))

        c.qs = Ctr(en(nc.semaphore("qs")), 16)
        c.qg = Ctr(en(nc.semaphore("qg")), 16)
        c.qa = Ctr(en(nc.semaphore("qa")), 16)
        c.cv = Ctr(en(nc.semaphore("cv")))
        c.cs = Ctr(en(nc.semaphore("cs")))
        c.ct = Ctr(en(nc.semaphore("ct")))
        c.cp = Ctr(en(nc.semaphore("cp")))
        c.cy = Ctr(en(nc.semaphore("cy")))
        c.qv = [Ctr(en(nc.semaphore(f"qv{i}")), 16) for i in range(4)]
        c.pollsems = [en(nc.semaphore(f"pl{i}")) for i in range(4)]

        c.block = en(nc.Block())

        emit_phase0(c)
        emit_ternary(c)
        emit_phaseA(c)
        if stage >= 2:
            emit_exchange_ssm(c)
            emit_bprep(c)
        if stage >= 3:
            emit_scan(c)
        if stage >= 4:
            emit_phaseC(c)
        emit_dbg(c)
        emit_tail(c)

    return nc


def emit_poll(c, sy, sem, flag_tensor, word, tag):
    with sy.register(f"pf{tag}") as f, sy.register(f"pi{tag}") as it, \
            sy.register(f"pn{tag}") as nv:
        sy.reg_load(nv, c.s_nonce[0:1, 0:1])
        if word:
            sy.reg_add(nv, nv, word)
        sy.reg_mov(f, 0)
        sy.reg_mov(it, 0)
        with sy.While(s_logical_and(make_scalar_value(f) != make_scalar_value(nv),
                                    make_scalar_value(it) < POLL_CAP)):
            sy.dma_start(out=c.s_flagrd[:, :],
                         in_=flag_tensor[0:1, word:word + 1]).then_inc(sem, 16)
            sy.reg_add(it, it, 1)
            sy.wait_ge(sem, make_scalar_value(it) * 16)
            sy.reg_load(f, c.s_flagrd[0:1, 0:1])


def emit_flag(c, sy, word):
    c.flagseq = getattr(c, "flagseq", 0) + 1
    with sy.register(f"fw{word}_{c.flagseq}") as nv:
        sy.reg_load(nv, c.s_nonce[0:1, 0:1])
        if word:
            sy.reg_add(nv, nv, word)
        sy.reg_save(c.s_flags[0:1, word:word + 1], nv)
    with sy.register(f"fr{word}_{c.flagseq}") as role:
        sy.reg_load(role, c.s_role[0:1, 0:1])
        with sy.If(make_scalar_value(role) == 0):
            sy.dma_start(out=c.fl[0][0:1, :], in_=c.s_flags[0:1, :]).then_inc(c.qs.sem, 16)
        with sy.Else():
            sy.dma_start(out=c.fl[1][0:1, :], in_=c.s_flags[0:1, :]).then_inc(c.qs.sem, 16)
    c.qs.n += 16
    sy.wait_ge(c.qs.sem, c.qs.n)


def emit_phase0(c):
    block = c.block
    qs, qg, cv, cs = c.qs, c.qg, c.cv, c.cs

    @block.sync
    def _(sy):
        sy.dma_start(out=c.s_role[:, :], in_=c.role_h[:, :]).then_inc(qs.sem, 16)
        sy.dma_start(out=c.s_nonce[:, :], in_=c.nonce_h[:, :]).then_inc(qs.sem, 16)
        sy.dma_start(out=c.s_cst[:, 0:16], in_=c.convb[:, :]).then_inc(qs.sem, 16)
        sy.dma_start(out=c.s_cst[:, 16:32], in_=c.bdt[:, :]).then_inc(qs.sem, 16)
        sy.dma_start(out=c.s_cst[:, 32:48], in_=c.dskip[:, :]).then_inc(qs.sem, 16)
        sy.dma_start(out=c.s_cst[:, 48:64], in_=c.ffnb[:, :]).then_inc(qs.sem, 16)
        sy.dma_start(out=c.s_alogA[:, :], in_=c.alog[:, :]).then_inc(qs.sem, 16)
        sy.dma_start(out=c.s_convw.ap().rearrange("p (d k) -> p d k", d=NDT),
                     in_=c.convw.ap().rearrange("d p k -> p d k")).then_inc(qs.sem, 16)
    qs.n += 8 * 16

    @block.gpsimd
    def _(g):
        g = EW(g, c.cp)
        g.dma_start(out=c.s_inln[:, :], in_=c.inln_row[:, :]).then_inc(qg.sem, 16)
        g.dma_start(out=c.s_preln[:, :], in_=c.preln_row[:, :]).then_inc(qg.sem, 16)
        g.dma_start(out=c.s_dtln[:, :], in_=c.dtln_row[:, :]).then_inc(qg.sem, 16)
        g.dma_start(out=c.s_bcsel[:, :], in_=c.bc_sel[:, :]).then_inc(qg.sem, 16)
        g.dma_start(out=c.s_grpsel[:, :], in_=c.grp_sel[:, :]).then_inc(qg.sem, 16)
        g.dma_start(out=c.s_ident[:, :], in_=c.ident[:, :]).then_inc(qg.sem, 16)
        g.dma_start(out=c.s_wdtT.ap().rearrange("p (d q) -> p d q", d=NDT),
                    in_=c.w_dtT.ap().rearrange("d p q -> p d q")).then_inc(qg.sem, 16)
    qg.n += 7 * 16

    @block.vector
    def _(v):
        v = EW(v, c.cv)
        v.memset(c.s_onesc[:, :], 1.0)
        v.memset(c.s_hstate[:, :], 0.0)
        v.memset(c.s_kc[:, 0:1], 1e-5)
        v.memset(c.s_kc[:, 1:2], 1e-6)
        v.memset(c.s_kc[:, 2:3], 1.0)
        v.memset(c.s_stash[:, :], 0.0)

    @block.scalar
    def _(sc):
        sc = EW(sc, c.cs)
        sc.wait_ge(qs.sem, qs.n)
        sc.activation(c.s_alogA[:, :], c.s_alogA[:, :], AF.Exp)

    @block.vector
    def _(v):
        v = EW(v, c.cv)
        v.wait_ge(cs.sem, cs.n)
        v.tensor_scalar_mul(c.s_alogA[:, :], c.s_alogA[:, :], -1.0)

    @block.sync
    def _(sy):
        sy.wait_ge(qs.sem, qs.n)
        with sy.register("nv0") as nv0:
            sy.reg_load(nv0, c.s_nonce[0:1, 0:1])
            sy.reg_sub(nv0, nv0, 1)
            for i in range(16):
                sy.reg_save(c.s_flags[0:1, i:i + 1], nv0)


def emit_ternary(c):
    block = c.block
    qs, qa, qg, cv, cs, ct = c.qs, c.qa, c.qg, c.cv, c.cs, c.ct

    for m in range(NKH):
        @block.scalar
        def _(sc, m=m):
            sc = EW(sc, c.cs)
            sc.wait_ge(cv.sem, cv.n)
            sc.wait_ge(qs.sem, qs.n)
            sc.dma_start(out=c.af[:, 0:2048], in_=c.w_tern[m * 128:(m + 1) * 128, :]).then_inc(qa.sem, 16)
        qa.n += 16

        @block.vector
        def _(v, m=m):
            v = EW(v, c.cv)
            v.wait_ge(qa.sem, qa.n)
            v.reduce_sum(c.af[:, 2048:2049], c.af[:, 0:2048], AX.X, apply_absolute_value=True)
            v.tensor_scalar_mul(c.af[:, 2048:2049], c.af[:, 2048:2049], 0.7 / H)
            v.tensor_scalar_mul(c.af[:, 2049:2050], c.af[:, 2048:2049], -1.0)
            v.tensor_single_scalar(c.ar2[:, 4096:6144], c.af[:, 0:2048], c.af[:, 2048:2049], ALU.is_gt)
            v.tensor_single_scalar(c.ar2[:, 6144:8192], c.af[:, 0:2048], c.af[:, 2049:2050], ALU.is_lt)
            v.tensor_tensor(c.ar2[:, 0:2048], c.ar2[:, 4096:6144], c.ar2[:, 6144:8192], ALU.subtract)
            v.nop()

        for k in range(NKH):
            @block.tensor
            def _(t, m=m, k=k):
                t.wait_ge(cv.sem, cv.n)
                t.wait_ge(qg.sem, c.qg.n)
                t.transpose(c.pT[:, :], c.ar2[:, k * 128:(k + 1) * 128], c.s_ident[:, :]).then_inc(ct.sem, 1)
            ct.n += 1

            @block.vector
            def _(v, m=m, k=k):
                v = EW(v, c.cv)
                v.wait_ge(ct.sem, ct.n)
                v.wait_ge(qs.sem, qs.n)
                v.tensor_copy(c.ar2[:, 2048 + (k % 2) * 128:2048 + (k % 2) * 128 + 128], c.pT[:, :])
                v.nop()

            @block.sync
            def _(sy, m=m, k=k):
                sy.wait_ge(cv.sem, cv.n)
                sy.dma_start(out=c.qT_cache[k, m, :, :],
                             in_=c.ar2[:, 2048 + (k % 2) * 128:2048 + (k % 2) * 128 + 128]).then_inc(qs.sem, 16)
            qs.n += 16


def emit_phaseA(c):
    block = c.block
    qs, qg, qa, cv, cs, ct = c.qs, c.qg, c.qa, c.cv, c.cs, c.ct

    for st_ in range(16):
        col0 = st_ * 256
        for j in range(NKH):
            @block.scalar
            def _(sc, st_=st_, j=j, col0=col0):
                sc = EW(sc, c.cs)
                sc.wait_ge(cv.sem, cv.n)
                sc.dma_start(out=c.af[:, 0:256],
                             in_=c.xT[j * 128:(j + 1) * 128, col0:col0 + 256]).then_inc(qa.sem, 16)
            qa.n += 16

            @block.vector
            def _(v, st_=st_, j=j):
                v = EW(v, c.cv)
                v.wait_ge(qa.sem, qa.n)
                if j > 0:
                    v.wait_ge(ct.sem, ct.n)
                v.tensor_copy(c.ar2[:, 0:256], c.af[:, 0:256])
                v.tensor_tensor(c.ar2[:, 256:512], c.ar2[:, 0:256], c.ar2[:, 0:256], ALU.mult)
                v.nop()

            @block.tensor
            def _(t, st_=st_, j=j):
                t.wait_ge(cv.sem, cv.n)
                t.matmul(c.pst[0:1, 0:256], c.s_onesc[:, :], c.ar2[:, 0:256],
                         start=(j == 0), stop=(j == NKH - 1))
                t.matmul(c.p[5][0:1, 0:256], c.s_onesc[:, :], c.ar2[:, 256:512],
                         start=(j == 0), stop=(j == NKH - 1)).then_inc(ct.sem, 1)
            ct.n += 1

        @block.vector
        def _(v, st_=st_):
            v = EW(v, c.cv)
            v.wait_ge(ct.sem, ct.n)
            v.wait_ge(qs.sem, qs.n)
            v.tensor_scalar_mul(c.s_rowf[0:1, 0:256], c.pst[0:1, 0:256], 1.0 / H)
            v.tensor_scalar_mul(c.s_rowf[0:1, 512:768], c.p[5][0:1, 0:256], 1.0 / H)
            v.tensor_tensor(c.s_rowf[0:1, 1024:1280], c.s_rowf[0:1, 0:256], c.s_rowf[0:1, 0:256], ALU.mult)
            v.tensor_tensor(c.s_rowf[0:1, 512:768], c.s_rowf[0:1, 512:768], c.s_rowf[0:1, 1024:1280], ALU.subtract)
            v.nop()

        @block.scalar
        def _(sc, st_=st_):
            sc = EW(sc, c.cs)
            sc.wait_ge(cv.sem, cv.n)
            sc.activation(c.s_rowf[0:1, 1024:1280], c.s_rowf[0:1, 512:768], AF.Ln, bias=1e-5)
            sc.activation(c.s_rowf[0:1, 512:768], c.s_rowf[0:1, 1024:1280], AF.Exp, scale=-0.5)
            sc.nop()

        @block.vector
        def _(v, st_=st_):
            v = EW(v, c.cv)
            v.wait_ge(cs.sem, cs.n)
            v.tensor_tensor(c.s_rowf[0:1, 1024:1280], c.s_rowf[0:1, 0:256], c.s_rowf[0:1, 512:768], ALU.mult)
            v.tensor_scalar_mul(c.s_rowf[0:1, 1024:1280], c.s_rowf[0:1, 1024:1280], -1.0)
            v.nop()

        @block.sync
        def _(sy, st_=st_, col0=col0):
            sy.wait_ge(cv.sem, cv.n)
            sy.dma_start(out=c.stats_scr[0:1, col0:col0 + 256], in_=c.s_rowf[0:1, 512:768]).then_inc(qs.sem, 16)
            sy.dma_start(out=c.stats_scr[1:2, col0:col0 + 256], in_=c.s_rowf[0:1, 1024:1280]).then_inc(qs.sem, 16)
        qs.n += 32

    for st_ in range(8):
        col0 = st_ * 512

        @block.sync
        def _(sy, st_=st_, col0=col0):
            sy.wait_ge(qs.sem, qs.n)
            sy.wait_ge(cv.sem, cv.n)
            sy.dma_start(out=c.af[:, 512:1024],
                         in_=bass.AP(c.stats_scr, 0 * T + col0, [[0, 128], [1, 512]])).then_inc(qs.sem, 16)
            sy.dma_start(out=c.af[:, 1024:1536],
                         in_=bass.AP(c.stats_scr, 1 * T + col0, [[0, 128], [1, 512]])).then_inc(qs.sem, 16)
        qs.n += 32

        for j in range(NKH):
            @block.scalar
            def _(sc, st_=st_, j=j, col0=col0):
                sc = EW(sc, c.cs)
                sc.wait_ge(cv.sem, cv.n)
                sc.dma_start(out=c.af[:, 0:512],
                             in_=c.xT[j * 128:(j + 1) * 128, col0:col0 + 512]).then_inc(qa.sem, 16)
            qa.n += 16

            @block.vector
            def _(v, st_=st_, j=j):
                v = EW(v, c.cv)
                v.wait_ge(qa.sem, qa.n)
                v.wait_ge(qs.sem, qs.n)
                if j > 0:
                    v.wait_ge(ct.sem, ct.n)
                v.tensor_tensor(c.af[:, 1536:2048], c.af[:, 0:512], c.af[:, 512:1024], ALU.mult)
                v.tensor_tensor(c.ar2[:, 0:512], c.af[:, 1536:2048], c.af[:, 1024:1536], ALU.add)
                v.tensor_tensor(c.ar2[:, 512:1024], c.ar2[:, 0:512], c.ar2[:, 0:512], ALU.mult)
                v.nop()

            @block.sync
            def _(sy, st_=st_, j=j, col0=col0):
                sy.wait_ge(cv.sem, cv.n)
                sy.dma_start(out=c.xn_spill[j, :, col0:col0 + 512], in_=c.ar2[:, 0:512]).then_inc(qs.sem, 16)
            qs.n += 16

            @block.tensor
            def _(t, st_=st_, j=j):
                t.wait_ge(cv.sem, cv.n)
                t.matmul(c.pst[0:1, :], c.s_onesc[:, :], c.ar2[:, 512:1024],
                         start=(j == 0), stop=(j == NKH - 1)).then_inc(ct.sem, 1)
            ct.n += 1

        @block.scalar
        def _(sc, st_=st_, col0=col0):
            sc = EW(sc, c.cs)
            sc.wait_ge(ct.sem, ct.n)
            sc.activation(c.s_rowf[0:1, 0:512], c.pst[0:1, :], AF.Ln, bias=1e-6, scale=1.0 / H)
            sc.activation(c.s_rowf[0:1, 0:512], c.s_rowf[0:1, 0:512], AF.Exp, scale=-0.5)
            sc.nop()

        @block.sync
        def _(sy, st_=st_, col0=col0):
            sy.wait_ge(cs.sem, cs.n)
            sy.dma_start(out=c.stats_scr[2:3, col0:col0 + 512], in_=c.s_rowf[0:1, 0:512]).then_inc(qs.sem, 16)
        qs.n += 16

    c.wbuf_use = [ct.n, ct.n]
    for th in range(2):
        th0 = th * 2048
        for j in range(NKH):
            @block.sync
            def _(sy, th=th, j=j, th0=th0):
                sy.wait_ge(cv.sem, cv.n)
                sy.wait_ge(qs.sem, qs.n)
                sy.dma_start(out=c.ar2[:, 0:2048], in_=c.xn_spill[j, :, th0:th0 + 2048]).then_inc(qs.sem, 16)
            qs.n += 16

            for tq in range(4):
                @block.sync
                def _(sy, th=th, j=j, tq=tq, th0=th0):
                    if j == 0 or True:
                        sy.wait_ge(cv.sem, cv.n)
                        sy.dma_start(out=c.s_rowf[0:1, 1536:2048],
                                     in_=c.stats_scr[2:3, th0 + tq * 512:th0 + (tq + 1) * 512]).then_inc(qs.sem, 16)
                qs.n += 16

                @block.vector
                def _(v, th=th, j=j, tq=tq):
                    v = EW(v, c.cv)
                    v.wait_ge(qs.sem, qs.n)
                    v.wait_ge(ct.sem, ct.n)
                    v.tensor_copy(c.s_rowt[0:1, 0:512], c.s_rowf[0:1, 1536:2048])
                    v.nop()

                @block.tensor
                def _(t, th=th, j=j, tq=tq):
                    t.wait_ge(cv.sem, cv.n)
                    t.wait_ge(qg.sem, qg.n)
                    t.matmul(c.p[2][:, :], c.s_inln[0:1, j * 128:(j + 1) * 128],
                             c.s_rowt[0:1, 0:512], start=True, stop=True).then_inc(ct.sem, 1)
                ct.n += 1

                @block.vector
                def _(v, th=th, j=j, tq=tq):
                    v = EW(v, c.cv)
                    v.wait_ge(ct.sem, ct.n)
                    v.tensor_tensor(c.ar1[:, j * 2048 + tq * 512:j * 2048 + (tq + 1) * 512],
                                    c.ar2[:, tq * 512:(tq + 1) * 512], c.p[2][:, :], ALU.mult)
                    v.nop()

        for dt_ in range(32):
            buf = dt_ % 2
            wb = buf * 2048

            @block.gpsimd
            def _(g, th=th, dt_=dt_, buf=buf, wb=wb):
                g = EW(g, c.cp)
                g.wait_ge(ct.sem, c.wbuf_use[buf])
                g.wait_ge(cv.sem, cv.n)
                g.dma_start(out=c.ar2[:, wb:wb + 2048].rearrange("p (k q) -> p k q", k=NKH),
                            in_=c.w_inT.ap()[dt_].rearrange("k p q -> p k q")).then_inc(qg.sem, 16)
            qg.n += 16

            for tq in range(4):
                pb = c.p[tq % 2]

                @block.tensor
                def _(t, th=th, dt_=dt_, tq=tq, wb=wb, th0=th0, pb=pb):
                    t.wait_ge(qg.sem, qg.n)
                    t.wait_ge(cv.sem, cv.n)
                    t.wait_ge(cs.sem, cs.n)
                    last = None
                    for k in range(NKH):
                        last = t.matmul(pb[:, :], c.ar2[:, wb + k * 128:wb + (k + 1) * 128],
                                        c.ar1[:, k * 2048 + tq * 512:k * 2048 + (tq + 1) * 512],
                                        start=(k == 0), stop=(k == NKH - 1))
                    last.then_inc(ct.sem, 1)
                ct.n += 1
                emit_phaseA_evac(c, th, dt_, tq, pb)
            c.wbuf_use[buf] = ct.n


def emit_phaseA_evac(c, th, dt_, tq, pb):
    block = c.block
    qs, qg, cv, cs, ct = c.qs, c.qg, c.cv, c.cs, c.ct
    col = th * 2048 + tq * 512
    is_u = dt_ < NDT

    if is_u:
        @block.scalar
        def _(sc):
            sc = EW(sc, c.cs)
            sc.wait_ge(ct.sem, ct.n)
            sc.wait_ge(cv.sem, cv.n)
            sc.activation(c.s_upre[:, 3:515], pb[:, :], AF.Copy)
            sc.nop()

        @block.vector
        def _(v):
            v = EW(v, c.cv)
            v.wait_ge(cs.sem, cs.n)
            if th == 0 and tq == 0:
                v.memset(c.s_upre[:, 0:3], 0.0)
            else:
                v.tensor_copy(c.s_upre[:, 0:3], c.s_stash[:, dt_ * 3:dt_ * 3 + 3])
            v.tensor_single_scalar(c.af[:, 4608:5120], c.s_upre[:, 0:512],
                                   c.s_convw[:, dt_ * 4:dt_ * 4 + 1], ALU.mult)
            for k in range(1, 4):
                v.scalar_tensor_tensor(c.af[:, 4608:5120], c.s_upre[:, k:k + 512],
                                       c.s_convw[:, dt_ * 4 + k:dt_ * 4 + k + 1],
                                       c.af[:, 4608:5120], ALU.mult, ALU.add)
            v.tensor_copy(c.s_stash[:, dt_ * 3:dt_ * 3 + 3], c.s_upre[:, 512:515])
            v.nop()

        @block.scalar
        def _(sc):
            sc = EW(sc, c.cs)
            sc.wait_ge(cv.sem, cv.n)
            sc.activation(c.af[:, 4096:4608], c.af[:, 4608:5120], AF.Silu,
                          bias=c.s_cst[:, dt_:dt_ + 1])
            sc.nop()

        @block.vector
        def _(v):
            v = EW(v, c.cv)
            v.wait_ge(cs.sem, cs.n)
            v.wait_ge(qs.sem, qs.n)
            v.wait_ge(ct.sem, ct.n)
            v.tensor_copy(c.ar2[:, 4096:4608], c.af[:, 4096:4608])
            v.nop()

        @block.sync
        def _(sy):
            sy.wait_ge(cv.sem, cv.n)
            sy.dma_start(out=c.u_spill[dt_, :, col:col + 512], in_=c.ar2[:, 4096:4608]).then_inc(qs.sem, 16)
        qs.n += 16

        @block.gpsimd
        def _(g):
            g = EW(g, c.cp)
            g.wait_ge(ct.sem, ct.n)
            g.dma_start(out=c.ar2[:, 4608:4768], in_=c.w_xT.ap()[dt_]).then_inc(qg.sem, 16)
        qg.n += 16

        @block.tensor
        def _(t):
            t.wait_ge(qg.sem, qg.n)
            t.wait_ge(cv.sem, cv.n)
            t.wait_ge(cv.sem, cv.n)
            t.matmul(c.p[3][:, :], c.ar2[:, 4608:4736], c.ar2[:, 4096:4608], start=True, stop=True)
            t.matmul(c.p[4][0:32, :], c.ar2[:, 4736:4768], c.ar2[:, 4096:4608], start=True, stop=True).then_inc(ct.sem, 1)
        ct.n += 1

        @block.vector
        def _(v):
            v = EW(v, c.cv)
            v.wait_ge(ct.sem, ct.n)
            sl = slice(col, col + 512)
            if dt_ == 0:
                v.tensor_copy(c.ssm1[:, sl], c.p[3][:, :])
                v.tensor_copy(c.ssmA[0:32, sl], c.p[4][0:32, :])
            else:
                v.tensor_tensor(c.ssm1[:, sl], c.ssm1[:, sl], c.p[3][:, :], ALU.add)
                v.tensor_tensor(c.ssmA[0:32, sl], c.ssmA[0:32, sl], c.p[4][0:32, :], ALU.add)
            v.nop()
    else:
        @block.scalar
        def _(sc):
            sc = EW(sc, c.cs)
            sc.wait_ge(ct.sem, ct.n)
            sc.wait_ge(cv.sem, cv.n)
            sc.activation(c.af[:, 4096:4608], pb[:, :], AF.Silu)
            sc.nop()

        @block.vector
        def _(v):
            v = EW(v, c.cv)
            v.wait_ge(cs.sem, cs.n)
            v.wait_ge(qs.sem, qs.n)
            v.tensor_copy(c.ar2[:, 4096:4608], c.af[:, 4096:4608])
            v.nop()

        @block.sync
        def _(sy):
            sy.wait_ge(cv.sem, cv.n)
            sy.dma_start(out=c.sg_spill[dt_ - NDT, :, col:col + 512], in_=c.ar2[:, 4096:4608]).then_inc(qs.sem, 16)
        qs.n += 16


def emit_exchange_ssm(c):
    if c.pairless:
        return
    block = c.block
    qs, cv = c.qs, c.cv

    @block.sync
    def _(sy):
        sy.wait_ge(cv.sem, cv.n)
        with sy.register("exr") as role:
            sy.reg_load(role, c.s_role[0:1, 0:1])
            with sy.If(make_scalar_value(role) == 0):
                sy.dma_start(out=c.sh_ssm[0][0:128, :], in_=c.ssm1[:, :]).then_inc(qs.sem, 16)
                sy.dma_start(out=c.sh_ssm[0][128:160, :], in_=c.ssmA[0:32, :]).then_inc(qs.sem, 16)
            with sy.Else():
                sy.dma_start(out=c.sh_ssm[1][0:128, :], in_=c.ssm1[:, :]).then_inc(qs.sem, 16)
                sy.dma_start(out=c.sh_ssm[1][128:160, :], in_=c.ssmA[0:32, :]).then_inc(qs.sem, 16)
        qs.n += 32
        sy.wait_ge(qs.sem, qs.n)
        emit_flag(c, sy, 0)
        with sy.register("exr2") as role2:
            sy.reg_load(role2, c.s_role[0:1, 0:1])
            with sy.If(make_scalar_value(role2) == 0):
                emit_poll(c, sy, c.pollsems[0], c.fl[1], 0, "sA")
            with sy.Else():
                emit_poll(c, sy, c.pollsems[0], c.fl[0], 0, "sB")

    for pc in range(4):
        cl = pc * 1024

        @block.sync
        def _(sy, pc=pc, cl=cl):
            if pc > 0:
                sy.wait_ge(cv.sem, cv.n)
            with sy.register(f"exp{pc}") as role:
                sy.reg_load(role, c.s_role[0:1, 0:1])
                with sy.If(make_scalar_value(role) == 0):
                    sy.dma_start(out=c.af[:, 0:1024], in_=c.sh_ssm[1][0:128, cl:cl + 1024]).then_inc(qs.sem, 16)
                    sy.dma_start(out=c.af[0:32, 1024:2048], in_=c.sh_ssm[1][128:160, cl:cl + 1024]).then_inc(qs.sem, 16)
                with sy.Else():
                    sy.dma_start(out=c.af[:, 0:1024], in_=c.sh_ssm[0][0:128, cl:cl + 1024]).then_inc(qs.sem, 16)
                    sy.dma_start(out=c.af[0:32, 1024:2048], in_=c.sh_ssm[0][128:160, cl:cl + 1024]).then_inc(qs.sem, 16)
            qs.n += 32
            sy.wait_ge(qs.sem, qs.n)
            sy.nop().then_inc(c.cy.sem, 1)
        c.cy.n += 1

        @block.vector
        def _(v, pc=pc, cl=cl):
            v = EW(v, c.cv)
            v.wait_ge(c.cy.sem, c.cy.n)
            v.tensor_tensor(c.ssm1[:, cl:cl + 1024], c.ssm1[:, cl:cl + 1024], c.af[:, 0:1024], ALU.add)
            v.tensor_tensor(c.ssmA[0:32, cl:cl + 1024], c.ssmA[0:32, cl:cl + 1024],
                            c.af[0:32, 1024:2048], ALU.add)


def emit_bprep(c):
    block = c.block
    qs, cv, cs, ct = c.qs, c.cv, c.cs, c.ct

    for st_ in range(8):
        sl = slice(st_ * 512, (st_ + 1) * 512)

        @block.vector
        def _(v, sl=sl):
            v = EW(v, c.cv)
            v.wait_ge(ct.sem, ct.n)
            v.tensor_copy(c.s_trn[:, sl], c.ssm1[:, sl])
            v.tensor_tensor(c.ar2[:, 0:512], c.s_trn[:, sl], c.s_trn[:, sl], ALU.mult)

        @block.tensor
        def _(t, sl=sl):
            t.wait_ge(cv.sem, cv.n)
            t.wait_ge(cs.sem, cs.n)
            t.matmul(c.pst[0:1, :], c.s_onesc[:, :], c.ar2[:, 0:512],
                     start=True, stop=True).then_inc(ct.sem, 1)
        ct.n += 1

        @block.scalar
        def _(sc, sl=sl):
            sc = EW(sc, c.cs)
            sc.wait_ge(ct.sem, ct.n)
            sc.wait_ge(cv.sem, cv.n)
            sc.activation(c.s_rowf[0:1, 0:512], c.pst[0:1, :], AF.Ln, bias=1e-6, scale=1.0 / 128)
            sc.activation(c.s_rowf[0:1, 0:512], c.s_rowf[0:1, 0:512], AF.Exp, scale=-0.5)

        @block.vector
        def _(v, sl=sl):
            v = EW(v, c.cv)
            v.wait_ge(cs.sem, cs.n)
            v.tensor_copy(c.s_rowt[0:1, 0:512], c.s_rowf[0:1, 0:512])

        @block.tensor
        def _(t, sl=sl):
            t.wait_ge(cv.sem, cv.n)
            t.matmul(c.p[2][:, :], c.s_dtln[0:1, :], c.s_rowt[0:1, 0:512],
                     start=True, stop=True).then_inc(ct.sem, 1)
        ct.n += 1

        @block.vector
        def _(v, sl=sl):
            v = EW(v, c.cv)
            v.wait_ge(ct.sem, ct.n)
            v.tensor_tensor(c.s_trn[:, sl], c.ssm1[:, sl], c.p[2][:, :], ALU.mult)

        @block.vector
        def _(v, sl=sl):
            v = EW(v, c.cv)
            v.tensor_copy(c.s_sq2[0:32, :], c.ssmA[0:32, sl])
            v.tensor_tensor(c.s_sq2[0:32, :], c.s_sq2[0:32, :], c.s_sq2[0:32, :], ALU.mult)

        @block.tensor
        def _(t, sl=sl):
            t.wait_ge(cv.sem, cv.n)
            t.matmul(c.p[2][0:2, :], c.s_grpsel[:, :], c.s_sq2[0:32, :],
                     start=True, stop=True).then_inc(ct.sem, 1)
        ct.n += 1

        @block.scalar
        def _(sc, sl=sl):
            sc = EW(sc, c.cs)
            sc.wait_ge(ct.sem, ct.n)
            sc.activation(c.s_row2[0:2, :], c.p[2][0:2, :], AF.Ln, bias=1e-6, scale=1.0 / 16)
            sc.activation(c.s_row2[0:2, :], c.s_row2[0:2, :], AF.Exp, scale=-0.5)

        @block.vector
        def _(v, sl=sl):
            v = EW(v, c.cv)
            v.wait_ge(cs.sem, cs.n)
            v.tensor_copy(c.s_rowt[0:2, 0:512], c.s_row2[0:2, :])

        @block.tensor
        def _(t, sl=sl):
            t.wait_ge(cv.sem, cv.n)
            t.matmul(c.p[2][0:32, :], c.s_bcsel[:, :], c.s_rowt[0:2, 0:512],
                     start=True, stop=True).then_inc(ct.sem, 1)
        ct.n += 1

        @block.vector
        def _(v, sl=sl):
            v = EW(v, c.cv)
            v.wait_ge(ct.sem, ct.n)
            v.wait_ge(qs.sem, qs.n)
            v.tensor_tensor(c.s_bcn[0:32, :], c.ssmA[0:32, sl], c.p[2][0:32, :], ALU.mult)

        @block.sync
        def _(sy, sl=sl):
            sy.wait_ge(cv.sem, cv.n)
            sy.dma_start(out=c.bc_scr[:, sl], in_=c.s_bcn[0:32, :]).then_inc(qs.sem, 16)
        qs.n += 16


def emit_scan(c):
    block = c.block
    qs, qg, qa, cv, cs, ct, cp = c.qs, c.qg, c.qa, c.cv, c.cs, c.ct, c.cp
    BB, CB = 0, 16384
    DTc, DTU, UB, SGB, DBU, HB, CH0, CH1, YG = (0, 1024, 2048, 3072, 4096,
                                                5120, 6144, 7168, 8192)
    DA0, DA1, T2, YACC, Y2 = 0, 1024, 2048, 3072, 4096

    for kch in range(4):
        t0 = kch * TCH
        for n in range(16):
            @block.sync
            def _(sy, kch=kch, n=n, t0=t0):
                if n == 0:
                    sy.wait_ge(cv.sem, cv.n)
                    sy.wait_ge(qs.sem, qs.n)
                sy.dma_start(out=c.ar1[:, BB + n * 1024:BB + (n + 1) * 1024],
                             in_=bass.AP(c.bc_scr, n * T + t0, [[0, 128], [1, 1024]])).then_inc(qs.sem, 16)
                sy.dma_start(out=c.ar1[:, CB + n * 1024:CB + (n + 1) * 1024],
                             in_=bass.AP(c.bc_scr, (16 + n) * T + t0, [[0, 128], [1, 1024]])).then_inc(qs.sem, 16)
            qs.n += 32

        for d in range(NDT):
            @block.tensor
            def _(t, kch=kch, d=d, t0=t0):
                t.wait_ge(cv.sem, cv.n)
                t.wait_ge(cs.sem, cs.n)
                t.matmul(c.p[0][:, :], c.s_wdtT[:, d * 128:(d + 1) * 128],
                         c.s_trn[:, t0:t0 + 512], start=True, stop=True)
                t.matmul(c.p[1][:, :], c.s_wdtT[:, d * 128:(d + 1) * 128],
                         c.s_trn[:, t0 + 512:t0 + 1024], start=True, stop=True).then_inc(ct.sem, 1)
            ct.n += 1

            for tq in range(2):
                @block.scalar
                def _(sc, kch=kch, d=d, tq=tq):
                    sc = EW(sc, c.cs)
                    sc.wait_ge(ct.sem, ct.n)
                    sc.wait_ge(cv.sem, cv.n)
                    sc.activation(c.af[:, T2:T2 + 512], c.p[tq][:, :], AF.Relu,
                                  bias=c.s_cst[:, 16 + d:16 + d + 1])
                    sc.activation(c.af[:, T2 + 512:T2 + 1024], c.p[tq][:, :], AF.Abs,
                                  bias=c.s_cst[:, 16 + d:16 + d + 1])
                    sc.activation(c.af[:, T2 + 512:T2 + 1024], c.af[:, T2 + 512:T2 + 1024], AF.Exp, scale=-1.0)
                    sc.activation(c.af[:, T2 + 512:T2 + 1024], c.af[:, T2 + 512:T2 + 1024], AF.Ln, bias=1.0)

                @block.vector
                def _(v, kch=kch, d=d, tq=tq):
                    v = EW(v, c.cv)
                    v.wait_ge(cs.sem, cs.n)
                    if d == 0 and tq == 0:
                        v.wait_ge(ct.sem, ct.n)
                    v.tensor_tensor(c.af[:, T2:T2 + 512], c.af[:, T2:T2 + 512],
                                    c.af[:, T2 + 512:T2 + 1024], ALU.add)
                    v.tensor_copy(c.ar2[:, DTc + tq * 512:DTc + (tq + 1) * 512], c.af[:, T2:T2 + 512])

            @block.scalar
            def _(sc, kch=kch, d=d, t0=t0):
                sc.wait_ge(cv.sem, cv.n)
                sc.wait_ge(qa.sem, qa.n)
                sc.dma_start(out=c.ar2[:, UB:UB + 1024], in_=c.u_spill[d, :, t0:t0 + 1024]).then_inc(qa.sem, 16)
                sc.dma_start(out=c.ar2[:, SGB:SGB + 1024], in_=c.sg_spill[d, :, t0:t0 + 1024]).then_inc(qa.sem, 16)
            qa.n += 32

            @block.vector
            def _(v, kch=kch, d=d):
                v = EW(v, c.cv)
                v.wait_ge(qa.sem, qa.n)
                v.wait_ge(cp.sem, cp.n)
                v.tensor_tensor(c.ar2[:, DTU:DTU + 1024], c.ar2[:, DTc:DTc + 1024],
                                c.ar2[:, UB:UB + 1024], ALU.mult)
                v.memset(c.af[:, YACC:YACC + 1024], 0.0)

            c.pool_chuse = getattr(c, "pool_chuse", [cp.n, cp.n])
            c.pool_chuse[0] = c.pool_chuse[1] = cp.n
            for n in range(16):
                dab = DA0 if n % 2 == 0 else DA1
                chb = CH0 if n % 2 == 0 else CH1

                @block.scalar
                def _(sc, kch=kch, d=d, n=n, dab=dab):
                    sc = EW(sc, c.cs)
                    sc.wait_ge(cv.sem, cv.n)
                    if d == 0 and n <= 1:
                        sc.wait_ge(qs.sem, qs.n)
                    sc.activation(c.af[:, dab:dab + 1024], c.ar2[:, DTc:DTc + 1024], AF.Exp,
                                  scale=c.s_alogA[:, d * 16 + n:d * 16 + n + 1])

                @block.vector
                def _(v, kch=kch, d=d, n=n, dab=dab, chb=chb):
                    v = EW(v, c.cv)
                    v.wait_ge(cs.sem, cs.n)
                    v.wait_ge(cp.sem, c.pool_chuse[n % 2])
                    v.tensor_tensor(c.ar2[:, DBU:DBU + 1024], c.ar2[:, DTU:DTU + 1024],
                                    c.ar1[:, BB + n * 1024:BB + (n + 1) * 1024], ALU.mult)
                    if kch == 0:
                        v.tensor_tensor_scan(c.ar2[:, HB:HB + 1024], c.af[:, dab:dab + 1024],
                                             c.ar2[:, DBU:DBU + 1024], 0.0, ALU.mult, ALU.add)
                    else:
                        v.tensor_tensor_scan(c.ar2[:, HB:HB + 1024], c.af[:, dab:dab + 1024],
                                             c.ar2[:, DBU:DBU + 1024],
                                             c.s_hstate[:, d * 16 + n:d * 16 + n + 1],
                                             ALU.mult, ALU.add)
                    v.tensor_copy(c.s_hstate[:, d * 16 + n:d * 16 + n + 1],
                                  c.ar2[:, HB + 1023:HB + 1024])
                    v.tensor_tensor(c.ar2[:, chb:chb + 1024], c.ar2[:, HB:HB + 1024],
                                    c.ar1[:, CB + n * 1024:CB + (n + 1) * 1024], ALU.mult)

                @block.gpsimd
                def _(g, kch=kch, d=d, n=n, chb=chb):
                    g = EW(g, c.cp)
                    g.wait_ge(cv.sem, cv.n)
                    g.tensor_tensor(c.af[:, YACC:YACC + 1024], c.af[:, YACC:YACC + 1024],
                                    c.ar2[:, chb:chb + 1024], ALU.add)
                c.pool_chuse[n % 2] = cp.n

            @block.vector
            def _(v, kch=kch, d=d):
                v = EW(v, c.cv)
                v.wait_ge(cp.sem, cp.n)
                v.scalar_tensor_tensor(c.af[:, Y2:Y2 + 1024], c.ar2[:, UB:UB + 1024],
                                       c.s_cst[:, 32 + d:32 + d + 1], c.af[:, YACC:YACC + 1024],
                                       ALU.mult, ALU.add)
                v.tensor_tensor(c.ar2[:, YG + d * 1024:YG + (d + 1) * 1024],
                                c.af[:, Y2:Y2 + 1024], c.ar2[:, SGB:SGB + 1024], ALU.mult)

        emit_wout(c, kch)


def emit_wout(c, kch):
    block = c.block
    qs, qg, cv, cs, ct = c.qs, c.qg, c.cv, c.cs, c.ct
    YG = 8192
    EV = 0
    c.woutbuf_use = getattr(c, "woutbuf_use", [ct.n, ct.n])

    for m in range(NKH):
        buf = m % 2
        wb = buf * 2048

        @block.gpsimd
        def _(g, kch=kch, m=m, wb=wb, buf=buf):
            g.wait_ge(ct.sem, c.woutbuf_use[buf])
            g.wait_ge(cv.sem, cv.n)
            g.wait_ge(qg.sem, qg.n)
            g.dma_start(out=c.ar2[:, wb:wb + 2048].rearrange("p (k q) -> p k q", k=NDT),
                        in_=c.w_outT.ap()[m].rearrange("k p q -> p k q")).then_inc(qg.sem, 16)
        qg.n += 16

        @block.tensor
        def _(t, kch=kch, m=m, wb=wb):
            t.wait_ge(qg.sem, qg.n)
            t.wait_ge(cv.sem, cv.n)
            t.wait_ge(cs.sem, cs.n)
            last = None
            for tq in range(2):
                pb = c.p[2 + tq]
                for k in range(NDT):
                    last = t.matmul(pb[:, :], c.ar2[:, wb + k * 128:wb + (k + 1) * 128],
                                    c.ar2[:, YG + k * 1024 + tq * 512:YG + k * 1024 + (tq + 1) * 512],
                                    start=(k == 0), stop=(k == NDT - 1))
            last.then_inc(ct.sem, 1)
        ct.n += 1
        c.woutbuf_use[buf] = ct.n

        @block.scalar
        def _(sc, kch=kch, m=m):
            sc = EW(sc, c.cs)
            sc.wait_ge(ct.sem, ct.n)
            sc.wait_ge(qs.sem, qs.n)
            sc.activation(c.af[:, EV:EV + 512], c.p[2][:, :], AF.Copy)
            sc.activation(c.af[:, EV + 512:EV + 1024], c.p[3][:, :], AF.Copy)

        @block.sync
        def _(sy, kch=kch, m=m):
            sy.wait_ge(cs.sem, cs.n)
            if True:
                with sy.register(f"wo{kch}_{m}") as role:
                    sy.reg_load(role, c.s_role[0:1, 0:1])
                    with sy.If(make_scalar_value(role) == (kch % 2)):
                        sy.dma_start(out=c.mixer_mine[kch // 2, m, :, :],
                                     in_=c.af[:, EV:EV + 1024]).then_inc(qs.sem, 16)
                    with sy.Else():
                        with sy.register(f"wo2{kch}_{m}") as role2:
                            sy.reg_load(role2, c.s_role[0:1, 0:1])
                            with sy.If(make_scalar_value(role2) == 0):
                                sy.dma_start(out=c.sh_mix[0][kch // 2, m, :, :],
                                             in_=c.af[:, EV:EV + 1024]).then_inc(qs.sem, 16)
                            with sy.Else():
                                sy.dma_start(out=c.sh_mix[1][kch // 2, m, :, :],
                                             in_=c.af[:, EV:EV + 1024]).then_inc(qs.sem, 16)
                qs.n += 16
            sy.wait_ge(qs.sem, qs.n)

    if not c.pairless:
        @block.sync
        def _(sy, kch=kch):
            emit_flag(c, sy, 1 + kch // 2)


def emit_phaseC(c):
    block = c.block
    qs, qg, qa, cv, cs, ct = c.qs, c.qg, c.qa, c.cv, c.cs, c.ct
    PM1, PM2, XC, HID, T1 = 0, 1024, 2048, 3072, 4096

    for s in range(2):
        if not c.pairless:
            @block.sync
            def _(sy, s=s):
                with sy.register(f"c0r{s}") as role:
                    sy.reg_load(role, c.s_role[0:1, 0:1])
                    with sy.If(make_scalar_value(role) == 0):
                        emit_poll(c, sy, c.pollsems[1 + s], c.fl[1], 1 + s, f"m{s}a")
                    with sy.Else():
                        emit_poll(c, sy, c.pollsems[1 + s], c.fl[0], 1 + s, f"m{s}b")

        @block.sync
        def _(sy, s=s):
            sy.wait_ge(cv.sem, cv.n)
            sy.wait_ge(qs.sem, qs.n)
            with sy.register(f"c0b{s}") as role:
                sy.reg_load(role, c.s_role[0:1, 0:1])
                with sy.If(make_scalar_value(role) == 0):
                    sy.dma_start(out=c.ssm1[:, 0:1024],
                                 in_=bass.AP(c.stats_scr, 0 * T + 2 * s * 1024, [[0, 128], [1, 1024]])).then_inc(qs.sem, 16)
                    sy.dma_start(out=c.ssm1[:, 1024:2048],
                                 in_=bass.AP(c.stats_scr, 1 * T + 2 * s * 1024, [[0, 128], [1, 1024]])).then_inc(qs.sem, 16)
                with sy.Else():
                    sy.dma_start(out=c.ssm1[:, 0:1024],
                                 in_=bass.AP(c.stats_scr, 0 * T + (2 * s + 1) * 1024, [[0, 128], [1, 1024]])).then_inc(qs.sem, 16)
                    sy.dma_start(out=c.ssm1[:, 1024:2048],
                                 in_=bass.AP(c.stats_scr, 1 * T + (2 * s + 1) * 1024, [[0, 128], [1, 1024]])).then_inc(qs.sem, 16)
            qs.n += 32
            sy.wait_ge(qs.sem, qs.n)
            sy.nop().then_inc(c.cy.sem, 1)
        c.cy.n += 1

        for j in range(NKH):
            @block.sync
            def _(sy, s=s, j=j):
                sy.wait_ge(c.cy.sem, c.cy.n)
                sy.wait_ge(cv.sem, cv.n)
                sy.wait_ge(qs.sem, qs.n)
                sy.dma_start(out=c.af[:, PM1:PM1 + 1024], in_=c.mixer_mine[s, j, :, :]).then_inc(qs.sem, 16)
                qs.n += 16
                if not c.pairless:
                    with sy.register(f"c1r{s}_{j}") as role:
                        sy.reg_load(role, c.s_role[0:1, 0:1])
                        with sy.If(make_scalar_value(role) == 0):
                            sy.dma_start(out=c.af[:, PM2:PM2 + 1024], in_=c.sh_mix[1][s, j, :, :]).then_inc(qs.sem, 16)
                            sy.dma_start(out=c.af[:, XC:XC + 1024],
                                         in_=c.xT[j * 128:(j + 1) * 128, (2 * s) * 1024:(2 * s + 1) * 1024]).then_inc(qs.sem, 16)
                        with sy.Else():
                            sy.dma_start(out=c.af[:, PM2:PM2 + 1024], in_=c.sh_mix[0][s, j, :, :]).then_inc(qs.sem, 16)
                            sy.dma_start(out=c.af[:, XC:XC + 1024],
                                         in_=c.xT[j * 128:(j + 1) * 128, (2 * s + 1) * 1024:(2 * s + 2) * 1024]).then_inc(qs.sem, 16)
                    qs.n += 32
                else:
                    with sy.register(f"c1r{s}_{j}") as role:
                        sy.reg_load(role, c.s_role[0:1, 0:1])
                        with sy.If(make_scalar_value(role) == 0):
                            sy.dma_start(out=c.af[:, XC:XC + 1024],
                                         in_=c.xT[j * 128:(j + 1) * 128, (2 * s) * 1024:(2 * s + 1) * 1024]).then_inc(qs.sem, 16)
                        with sy.Else():
                            sy.dma_start(out=c.af[:, XC:XC + 1024],
                                         in_=c.xT[j * 128:(j + 1) * 128, (2 * s + 1) * 1024:(2 * s + 2) * 1024]).then_inc(qs.sem, 16)
                    qs.n += 16
                sy.wait_ge(qs.sem, qs.n)
                sy.nop().then_inc(c.cy.sem, 1)
            c.cy.n += 1

            @block.vector
            def _(v, s=s, j=j):
                v = EW(v, c.cv)
                v.wait_ge(c.cy.sem, c.cy.n)
                v.wait_ge(ct.sem, ct.n)
                if c.pairless:
                    v.memset(c.af[:, PM2:PM2 + 1024], 0.0)
                v.tensor_tensor(c.af[:, T1:T1 + 1024], c.af[:, XC:XC + 1024], c.ssm1[:, 0:1024], ALU.mult)
                v.tensor_tensor(c.af[:, T1:T1 + 1024], c.af[:, T1:T1 + 1024], c.ssm1[:, 1024:2048], ALU.add)
                v.tensor_tensor(c.af[:, HID:HID + 1024], c.af[:, T1:T1 + 1024], c.af[:, PM1:PM1 + 1024], ALU.add)
                v.tensor_tensor(c.af[:, HID:HID + 1024], c.af[:, HID:HID + 1024], c.af[:, PM2:PM2 + 1024], ALU.add)
                v.tensor_copy(c.ar2[:, 0:1024], c.af[:, HID:HID + 1024])
                v.tensor_tensor(c.ar2[:, 1024:2048], c.ar2[:, 0:1024], c.ar2[:, 0:1024], ALU.mult)

            @block.sync
            def _(sy, s=s, j=j):
                sy.wait_ge(cv.sem, cv.n)
                sy.dma_start(out=c.hidden_spill[s, j, :, :], in_=c.af[:, HID:HID + 1024]).then_inc(qs.sem, 16)
            qs.n += 16

            @block.tensor
            def _(t, s=s, j=j):
                t.wait_ge(cv.sem, cv.n)
                t.matmul(c.pst[0:1, :], c.s_onesc[:, :], c.ar2[:, 1024:1536],
                         start=(j == 0), stop=(j == NKH - 1))
                t.matmul(c.p[5][0:1, :], c.s_onesc[:, :], c.ar2[:, 1536:2048],
                         start=(j == 0), stop=(j == NKH - 1)).then_inc(ct.sem, 1)
            ct.n += 1

        @block.scalar
        def _(sc, s=s):
            sc = EW(sc, c.cs)
            sc.wait_ge(ct.sem, ct.n)
            sc.activation(c.s_rowf[0:1, 0:512], c.pst[0:1, :], AF.Ln, bias=1e-6, scale=1.0 / H)
            sc.activation(c.s_rowf[0:1, 0:512], c.s_rowf[0:1, 0:512], AF.Exp, scale=-0.5)
            sc.activation(c.s_rowf[0:1, 512:1024], c.p[5][0:1, :], AF.Ln, bias=1e-6, scale=1.0 / H)
            sc.activation(c.s_rowf[0:1, 512:1024], c.s_rowf[0:1, 512:1024], AF.Exp, scale=-0.5)

        @block.vector
        def _(v, s=s):
            v = EW(v, c.cv)
            v.wait_ge(cs.sem, cs.n)
            v.tensor_copy(c.s_rowt[0:1, 0:512], c.s_rowf[0:1, 0:512])
            v.tensor_copy(c.s_rowt[0:1, 512:1024], c.s_rowf[0:1, 512:1024])

        for j in range(NKH):
            @block.sync
            def _(sy, s=s, j=j):
                sy.wait_ge(cv.sem, cv.n)
                sy.wait_ge(qs.sem, qs.n)
                sy.dma_start(out=c.af[:, HID:HID + 1024], in_=c.hidden_spill[s, j, :, :]).then_inc(qs.sem, 16)
                qs.n += 16
                with sy.register(f"c2r{s}_{j}") as role:
                    sy.reg_load(role, c.s_role[0:1, 0:1])
                    with sy.If(make_scalar_value(role) == 0):
                        sy.dma_start(out=c.ar1[:, 16384 + j * 1024:16384 + (j + 1) * 1024],
                                     in_=c.xn_spill[j, :, (2 * s) * 1024:(2 * s + 1) * 1024]).then_inc(qs.sem, 16)
                    with sy.Else():
                        sy.dma_start(out=c.ar1[:, 16384 + j * 1024:16384 + (j + 1) * 1024],
                                     in_=c.xn_spill[j, :, (2 * s + 1) * 1024:(2 * s + 2) * 1024]).then_inc(qs.sem, 16)
                qs.n += 16
                sy.wait_ge(qs.sem, qs.n)
                sy.nop().then_inc(c.cy.sem, 1)
            c.cy.n += 1

            for tq in range(2):
                @block.tensor
                def _(t, s=s, j=j, tq=tq):
                    t.wait_ge(cv.sem, cv.n)
                    t.matmul(c.p[2][:, :], c.s_preln[0:1, j * 128:(j + 1) * 128],
                             c.s_rowt[0:1, tq * 512:(tq + 1) * 512],
                             start=True, stop=True).then_inc(ct.sem, 1)
                ct.n += 1

                @block.vector
                def _(v, s=s, j=j, tq=tq):
                    v = EW(v, c.cv)
                    v.wait_ge(ct.sem, ct.n)
                    v.wait_ge(c.cy.sem, c.cy.n)
                    v.tensor_tensor(c.ar1[:, j * 1024 + tq * 512:j * 1024 + (tq + 1) * 512],
                                    c.af[:, HID + tq * 512:HID + (tq + 1) * 512],
                                    c.p[2][:, :], ALU.mult)

        c.ffnbuf_use = getattr(c, "ffnbuf_use", [ct.n, ct.n])
        c.ffnbuf_use[0] = c.ffnbuf_use[1] = ct.n
        for mf in range(NMF):
            buf = mf % 2
            wb = buf * 4096

            @block.gpsimd
            def _(g, s=s, mf=mf, wb=wb, buf=buf):
                g.wait_ge(ct.sem, c.ffnbuf_use[buf])
                g.wait_ge(qg.sem, qg.n)
                g.dma_start(out=c.ar2[:, wb:wb + 2048].rearrange("p (k q) -> p k q", k=NKH),
                            in_=c.w_gT.ap()[mf].rearrange("k p q -> p k q")).then_inc(qg.sem, 16)
                g.dma_start(out=c.ar2[:, wb + 2048:wb + 4096].rearrange("p (k q) -> p k q", k=NKH),
                            in_=c.w_uT.ap()[mf].rearrange("k p q -> p k q")).then_inc(qg.sem, 16)
            qg.n += 32

            @block.tensor
            def _(t, s=s, mf=mf, wb=wb):
                t.wait_ge(qg.sem, qg.n)
                t.wait_ge(cv.sem, cv.n)
                t.wait_ge(cs.sem, cs.n)
                last = None
                for tq in range(2):
                    for k in range(NKH):
                        t.matmul(c.p[tq][:, :], c.ar2[:, wb + k * 128:wb + (k + 1) * 128],
                                 c.ar1[:, k * 1024 + tq * 512:k * 1024 + (tq + 1) * 512],
                                 start=(k == 0), stop=(k == NKH - 1))
                        last = t.matmul(c.p[2 + tq][:, :], c.ar2[:, wb + 2048 + k * 128:wb + 2048 + (k + 1) * 128],
                                        c.ar1[:, k * 1024 + tq * 512:k * 1024 + (tq + 1) * 512],
                                        start=(k == 0), stop=(k == NKH - 1))
                last.then_inc(ct.sem, 1)
            ct.n += 1
            c.ffnbuf_use[buf] = ct.n

            @block.scalar
            def _(sc, s=s, mf=mf):
                sc = EW(sc, c.cs)
                sc.wait_ge(ct.sem, ct.n)
                sc.wait_ge(cv.sem, cv.n)
                sc.activation(c.af[:, 0:512], c.p[0][:, :], AF.Silu)
                sc.activation(c.af[:, 512:1024], c.p[1][:, :], AF.Silu)

            @block.vector
            def _(v, s=s, mf=mf):
                v = EW(v, c.cv)
                v.wait_ge(cs.sem, cs.n)
                v.wait_ge(qs.sem, qs.n)
                v.tensor_tensor(c.ar2[:, 8192:8704], c.af[:, 0:512], c.p[2][:, :], ALU.mult)
                v.tensor_tensor(c.ar2[:, 8704:9216], c.af[:, 512:1024], c.p[3][:, :], ALU.mult)

            @block.sync
            def _(sy, s=s, mf=mf):
                sy.wait_ge(cv.sem, cv.n)
                sy.dma_start(out=c.v_spill[mf, :, :], in_=c.ar2[:, 8192:9216]).then_inc(qs.sem, 16)
            qs.n += 16

        groups = [(0, 3), (3, 6), (6, 9), (9, 12), (12, 14), (14, 16)]
        for g0, g1 in groups:
            nm = g1 - g0
            @block.gpsimd
            def _(g, s=s, g0=g0, g1=g1):
                g.wait_ge(ct.sem, ct.n)
                g.wait_ge(cv.sem, cv.n)
                g.wait_ge(qg.sem, qg.n)
                g.wait_ge(qs.sem, qs.n)
                for mi in range(g0, g1):
                    g.dma_start(out=c.ar2[:, (mi - g0) * 5632:(mi - g0) * 5632 + 5632].rearrange("p (k q) -> p k q", k=NMF),
                                in_=c.w_dnT.ap()[mi].rearrange("k p q -> p k q")).then_inc(qg.sem, 16)
            qg.n += 16 * nm

            ct_base = ct.n
            qv0 = [q.n for q in c.qv]

            @block.scalar
            def _(sc, s=s, g0=g0, g1=g1, ct_base=ct_base):
                sc.wait_ge(cv.sem, cv.n)
                for i in range(4):
                    sc.wait_ge(c.qv[i].sem, c.qv[i].n)
                for k in range(NMF):
                    if k >= 4:
                        sc.wait_ge(ct.sem, ct_base + k - 3)
                    sc.dma_start(out=c.ar2[:, 16896 + (k % 4) * 1024:16896 + (k % 4) * 1024 + 1024],
                                 in_=c.v_spill[k, :, :]).then_inc(c.qv[k % 4].sem, 16)
            for k in range(NMF):
                c.qv[k % 4].n += 16

            @block.tensor
            def _(t, s=s, g0=g0, g1=g1, qv0=qv0):
                t.wait_ge(qg.sem, qg.n)
                for k in range(NMF):
                    t.wait_ge(c.qv[k % 4].sem, qv0[k % 4] + 16 * (k // 4 + 1))
                    vb = 16896 + (k % 4) * 1024
                    last = None
                    for mi in range(g0, g1):
                        for tq in range(2):
                            last = t.matmul(c.p[2 * (mi - g0) + tq][:, :],
                                            c.ar2[:, (mi - g0) * 5632 + k * 128:(mi - g0) * 5632 + (k + 1) * 128],
                                            c.ar2[:, vb + tq * 512:vb + (tq + 1) * 512],
                                            start=(k == 0), stop=(k == NMF - 1))
                    last.then_inc(ct.sem, 1)
            ct.n += NMF

            for mi in range(g0, g1):
                @block.sync
                def _(sy, s=s, mi=mi, g0=g0):
                    sy.wait_ge(ct.sem, ct.n)
                    sy.wait_ge(cv.sem, cv.n)
                    sy.wait_ge(qs.sem, qs.n)
                    sy.dma_start(out=c.af[:, HID:HID + 1024], in_=c.hidden_spill[s, mi, :, :]).then_inc(qs.sem, 16)
                    qs.n += 16
                    with sy.register(f"c4r{s}_{mi}") as role:
                        sy.reg_load(role, c.s_role[0:1, 0:1])
                        with sy.If(make_scalar_value(role) == 0):
                            sy.dma_start(out=c.af[:, XC:XC + 1024],
                                         in_=c.xT[mi * 128:(mi + 1) * 128, (2 * s) * 1024:(2 * s + 1) * 1024]).then_inc(qs.sem, 16)
                        with sy.Else():
                            sy.dma_start(out=c.af[:, XC:XC + 1024],
                                         in_=c.xT[mi * 128:(mi + 1) * 128, (2 * s + 1) * 1024:(2 * s + 2) * 1024]).then_inc(qs.sem, 16)
                    qs.n += 16
                    sy.wait_ge(qs.sem, qs.n)
                    sy.nop().then_inc(c.cy.sem, 1)
                c.cy.n += 1

                @block.vector
                def _(v, s=s, mi=mi, g0=g0):
                    v = EW(v, c.cv)
                    v.wait_ge(c.cy.sem, c.cy.n)
                    v.wait_ge(ct.sem, ct.n)
                    v.tensor_tensor(c.af[:, T1:T1 + 512], c.p[2 * (mi - g0)][:, :],
                                    c.af[:, HID:HID + 512], ALU.add)
                    v.tensor_tensor(c.af[:, T1 + 512:T1 + 1024], c.p[2 * (mi - g0) + 1][:, :],
                                    c.af[:, HID + 512:HID + 1024], ALU.add)

                @block.gpsimd
                def _(g, s=s, mi=mi):
                    g.wait_ge(cv.sem, cv.n)
                    g.wait_ge(qg.sem, qg.n)
                    g.dma_start(out=c.ar2[:, 20992 + 0:20992 + 2048].rearrange("p (k q) -> p k q", k=NKH),
                                in_=c.qT_cache.ap()[:, mi].rearrange("k p q -> p k q")).then_inc(qg.sem, 16)
                qg.n += 16

                @block.tensor
                def _(t, s=s, mi=mi, g0=g0):
                    t.wait_ge(qg.sem, qg.n)
                    t.wait_ge(cv.sem, cv.n)
                    last = None
                    for tq in range(2):
                        for k in range(NKH):
                            last = t.matmul(c.p[2 * (mi - g0) + tq][:, :],
                                            c.ar2[:, 20992 + k * 128:20992 + (k + 1) * 128],
                                            c.ar1[:, 16384 + k * 1024 + tq * 512:16384 + k * 1024 + (tq + 1) * 512],
                                            start=(k == 0), stop=(k == NKH - 1))
                    last.then_inc(ct.sem, 1)
                ct.n += 1

                @block.scalar
                def _(sc, s=s, mi=mi, g0=g0):
                    sc = EW(sc, c.cs)
                    sc.wait_ge(ct.sem, ct.n)
                    sc.activation(c.af[:, PM1:PM1 + 512], c.p[2 * (mi - g0)][:, :], AF.Silu,
                                  bias=c.s_cst[:, 48 + mi:48 + mi + 1])
                    sc.activation(c.af[:, PM1 + 512:PM1 + 1024], c.p[2 * (mi - g0) + 1][:, :], AF.Silu,
                                  bias=c.s_cst[:, 48 + mi:48 + mi + 1])

                @block.vector
                def _(v, s=s, mi=mi):
                    v = EW(v, c.cv)
                    v.wait_ge(cs.sem, cs.n)
                    v.tensor_tensor(c.af[:, PM2:PM2 + 1024], c.af[:, T1:T1 + 1024],
                                    c.af[:, PM1:PM1 + 1024], ALU.mult)
                    v.tensor_tensor(c.af[:, PM2:PM2 + 1024], c.af[:, PM2:PM2 + 1024],
                                    c.af[:, XC:XC + 1024], ALU.add)

                @block.sync
                def _(sy, s=s, mi=mi):
                    sy.wait_ge(cv.sem, cv.n)
                    sy.dma_start(out=c.out_fm[mi * 128:(mi + 1) * 128, s * 1024:(s + 1) * 1024],
                                 in_=c.af[:, PM2:PM2 + 1024]).then_inc(qs.sem, 16)
                qs.n += 16


def emit_dbg(c):
    block = c.block
    qs = c.qs
    srcs = {
        "u_spill": c.u_spill, "sg_spill": c.sg_spill, "xn_spill": c.xn_spill,
        "stats_scr": c.stats_scr, "bc_scr": c.bc_scr, "qT_cache": c.qT_cache,
        "mixer_mine": c.mixer_mine, "hidden_spill": c.hidden_spill,
        "v_spill": c.v_spill, "g_spill": c.g_spill,
    }
    sbuf_srcs = {"ssm1": c.ssm1, "ssm2": lambda: c.ssmA[0:32, :], "trn": c.s_trn}

    @block.sync
    def _(sy):
        sy.wait_ge(c.cv.sem, c.cv.n)
        sy.wait_ge(c.cs.sem, c.cs.n)
        sy.wait_ge(c.ct.sem, c.ct.n)
        sy.wait_ge(c.cp.sem, c.cp.n)
        sy.wait_ge(c.qa.sem, c.qa.n)
        sy.wait_ge(c.qg.sem, c.qg.n)
        sy.wait_ge(c.qs.sem, c.qs.n)
        for name, t in c.dbg.items():
            key = name[4:] if name.startswith("dbg_") else name
            if key in srcs:
                sy.dma_start(out=t.ap(), in_=srcs[key].ap()).then_inc(qs.sem, 16)
            elif key == "ssm1":
                sy.dma_start(out=t.ap(), in_=c.ssm1[:, :]).then_inc(qs.sem, 16)
            elif key == "ssm2":
                sy.dma_start(out=t.ap(), in_=c.ssmA[0:32, :]).then_inc(qs.sem, 16)
            elif key == "trn":
                sy.dma_start(out=t.ap(), in_=c.s_trn[:, :]).then_inc(qs.sem, 16)
            elif key == "hstate":
                sy.dma_start(out=t.ap(), in_=c.s_hstate[:, :]).then_inc(qs.sem, 16)
            c.qs.n += 16


def emit_tail(c):
    block = c.block

    @block.sync
    def _(sy):
        sy.wait_ge(c.qs.sem, c.qs.n)
        sy.wait_ge(c.qa.sem, c.qa.n)
        sy.wait_ge(c.qg.sem, c.qg.n)
        for q in c.qv:
            sy.wait_ge(q.sem, q.n)


def stage_inputs(inputs, core):
    b, r = core // 2, core % 2
    D = slice(r * DH, (r + 1) * DH)
    f32 = np.float32

    x = np.asarray(inputs["L0_attn_hidden_states"])[b]
    w_in = np.asarray(inputs["L0_attn_w_in"])
    wmy = np.concatenate([w_in[D], w_in[4096 + r * DH:4096 + (r + 1) * DH]], axis=0)
    w_inT = np.ascontiguousarray(
        wmy.T.reshape(NKH, 128, 2 * NDT, 128).transpose(2, 0, 1, 3))
    convw = np.ascontiguousarray(np.asarray(inputs["L0_attn_conv_w"])[D, 0, :].reshape(NDT, 128, 4))
    convb = np.ascontiguousarray(np.asarray(inputs["L0_attn_conv_b"])[D].reshape(NDT, 128).T)
    bdt = np.ascontiguousarray(np.asarray(inputs["L0_attn_b_dt"])[D].reshape(NDT, 128).T)
    dskip = np.ascontiguousarray(np.asarray(inputs["L0_attn_D_skip"])[D].reshape(NDT, 128).T)
    alogm = np.asarray(inputs["L0_attn_A_log"])[D]
    alog = np.ascontiguousarray(alogm.reshape(NDT, 128, N).transpose(1, 0, 2).reshape(128, NDT * N))
    w_xT = np.ascontiguousarray(np.asarray(inputs["L0_attn_w_x"])[:, D].T.reshape(NDT, 128, 160))
    w_dtT = np.ascontiguousarray(np.asarray(inputs["L0_attn_w_dt"])[D].T.reshape(128, NDT, 128).transpose(1, 0, 2))
    dtln_row = np.asarray(inputs["L0_attn_dt_ln_w"]).reshape(1, 128)
    bl = np.asarray(inputs["L0_attn_b_ln_w"])
    cl = np.asarray(inputs["L0_attn_c_ln_w"])
    bc_sel = np.zeros((2, 32), f32)
    bc_sel[0, 0:16] = bl
    bc_sel[1, 16:32] = cl
    grp_sel = np.zeros((32, 2), f32)
    grp_sel[0:16, 0] = 1.0
    grp_sel[16:32, 1] = 1.0
    inln_row = np.asarray(inputs["L0_attn_input_ln_w"]).reshape(1, H)
    preln_row = np.asarray(inputs["L0_attn_pre_ff_ln_w"]).reshape(1, H)
    ident = np.eye(128, dtype=f32)
    w_outT = np.ascontiguousarray(
        np.asarray(inputs["L0_attn_w_out"])[:, D].T.reshape(NDT, 128, NKH, 128).transpose(2, 0, 1, 3))
    w_gT = np.ascontiguousarray(
        np.asarray(inputs["L0_attn_w_gate"]).T.reshape(NKH, 128, NMF, 128).transpose(2, 0, 1, 3))
    w_uT = np.ascontiguousarray(
        np.asarray(inputs["L0_attn_w_up"]).T.reshape(NKH, 128, NMF, 128).transpose(2, 0, 1, 3))
    w_dnT = np.ascontiguousarray(
        np.asarray(inputs["L0_attn_w_down"]).T.reshape(NMF, 128, NKH, 128).transpose(2, 0, 1, 3))
    ffnb = np.ascontiguousarray(np.asarray(inputs["L0_ffn_bias"]).reshape(NKH, 128).T)

    return {
        "xT": np.ascontiguousarray(x.T.astype(f32)),
        "w_inT": w_inT.astype(f32), "convw": convw.astype(f32),
        "convb": convb.astype(f32), "bdt": bdt.astype(f32),
        "dskip": dskip.astype(f32), "alog": alog.astype(f32),
        "w_xT": w_xT.astype(f32), "w_dtT": w_dtT.astype(f32),
        "dtln_row": dtln_row.astype(f32), "bc_sel": bc_sel, "grp_sel": grp_sel,
        "inln_row": inln_row.astype(f32), "preln_row": preln_row.astype(f32),
        "ident": ident,
        "w_outT": w_outT.astype(f32), "w_gT": w_gT.astype(f32),
        "w_uT": w_uT.astype(f32), "w_dnT": w_dnT.astype(f32),
        "w_tern": np.ascontiguousarray(np.asarray(inputs["L0_ffn_weight"]).astype(f32)),
        "ffnb": ffnb.astype(f32),
        "role": np.array([[r]], np.int32),
    }


def kernel(**inputs):
    nc = build_program(stage=STAGE)
    nonce = np.int32((int(time.time_ns() // 1000) & 0x3FFFFFF0) | 8)
    in_maps = []
    for core in range(8):
        m = stage_inputs(inputs, core)
        m["nonce"] = np.array([[nonce]], np.int32)
        in_maps.append(m)
    res = run_bass_kernel_spmd(nc, in_maps, core_ids=list(range(8)))
    out = np.zeros((B_BATCH, T, H), np.float32)
    for core in range(8):
        b, r = core // 2, core % 2
        o = np.asarray(res.results[core]["out_fm"])
        for si in range(2):
            kt = 2 * si + r
            out[b, kt * TCH:(kt + 1) * TCH, :] = o[:, si * TCH:(si + 1) * TCH].T
    return out


B_BATCH = 4


# version 03
# speedup vs baseline: 1.1023x; 1.1023x over previous
import time
import numpy as np
import concourse.bass as bass
import concourse.mybir as mybir
from concourse.bass_utils import run_bass_kernel_spmd
from concourse.expressions_rust import make_scalar_value, s_logical_and

F32 = mybir.dt.float32
BF16 = mybir.dt.bfloat16
I32 = mybir.dt.int32
AF = mybir.ActivationFunctionType
ALU = mybir.AluOpType
AX = mybir.AxisListType

H = 2048
DH = 2048
N = 16
FF = 5632
T = 4096
NKH = 16
NDT = 16
NMF = 44
TCH = 1024
POLL_CAP = 3000000
STAGE = 4


class Ctr:
    def __init__(self, sem, step=1):
        self.sem = sem
        self.n = 0
        self.step = step


class Ctx:
    pass



_WRAP_OPS = {
    "tensor_tensor", "tensor_add", "tensor_sub", "tensor_mul", "tensor_max",
    "tensor_scalar", "tensor_single_scalar", "tensor_scalar_add",
    "tensor_scalar_sub", "tensor_scalar_mul", "tensor_scalar_max",
    "tensor_scalar_min", "tensor_relu", "tensor_reduce", "reduce_max",
    "reduce_sum", "scalar_tensor_tensor", "tensor_tensor_scan", "tensor_copy",
    "memset", "memzero", "activation", "sign", "square", "sqrt", "mul",
    "copy", "add", "select", "copy_predicated", "reciprocal", "transpose",
}


class EW:

    def __init__(self, eng, ctr):
        self._e = eng
        self._c = ctr

    def __getattr__(self, name):
        fn = getattr(self._e, name)
        if name not in _WRAP_OPS:
            return fn

        def call(*a, **k):
            self._e.wait_ge(self._c.sem, self._c.n)
            r = fn(*a, **k)
            r.then_inc(self._c.sem, 1)
            self._c.n += 1
            return r
        return call


def build_program(stage=STAGE, dbg=(), pairless=False):
    nc = bass.Bass("TRN2", target_bir_lowering=False, debug=False)
    c = Ctx()
    c.nc = nc
    c.stage = stage
    c.pairless = pairless

    c.xT = nc.dram_tensor("xT", [H, T], F32, kind="ExternalInput")
    c.w_inT = nc.dram_tensor("w_inT", [2 * NDT, NKH, 128, 128], F32, kind="ExternalInput")
    c.convw = nc.dram_tensor("convw", [NDT, 128, 4], F32, kind="ExternalInput")
    c.convb = nc.dram_tensor("convb", [128, NDT], F32, kind="ExternalInput")
    c.bdt = nc.dram_tensor("bdt", [128, NDT], F32, kind="ExternalInput")
    c.dskip = nc.dram_tensor("dskip", [128, NDT], F32, kind="ExternalInput")
    c.alog = nc.dram_tensor("alog", [128, NDT * N], F32, kind="ExternalInput")
    c.w_xT = nc.dram_tensor("w_xT", [NDT, 128, 160], F32, kind="ExternalInput")
    c.w_dtT = nc.dram_tensor("w_dtT", [NDT, 128, 128], F32, kind="ExternalInput")
    c.dtln_row = nc.dram_tensor("dtln_row", [1, 128], F32, kind="ExternalInput")
    c.bc_sel = nc.dram_tensor("bc_sel", [2, 32], F32, kind="ExternalInput")
    c.grp_sel = nc.dram_tensor("grp_sel", [32, 2], F32, kind="ExternalInput")
    c.inln_row = nc.dram_tensor("inln_row", [1, H], F32, kind="ExternalInput")
    c.preln_row = nc.dram_tensor("preln_row", [1, H], F32, kind="ExternalInput")
    c.ident = nc.dram_tensor("ident", [128, 128], F32, kind="ExternalInput")
    c.w_outT = nc.dram_tensor("w_outT", [NKH, NDT, 128, 128], F32, kind="ExternalInput")
    c.w_gT = nc.dram_tensor("w_gT", [NMF, NKH, 128, 128], F32, kind="ExternalInput")
    c.w_uT = nc.dram_tensor("w_uT", [NMF, NKH, 128, 128], F32, kind="ExternalInput")
    c.w_dnT = nc.dram_tensor("w_dnT", [NKH, NMF, 128, 128], F32, kind="ExternalInput")
    c.w_tern = nc.dram_tensor("w_tern", [H, H], F32, kind="ExternalInput")
    c.ffnb = nc.dram_tensor("ffnb", [128, NKH], F32, kind="ExternalInput")
    c.role_h = nc.dram_tensor("role", [1, 1], I32, kind="ExternalInput")
    c.nonce_h = nc.dram_tensor("nonce", [1, 1], I32, kind="ExternalInput")

    c.out_fm = nc.dram_tensor("out_fm", [H, 2 * TCH], F32, kind="ExternalOutput")
    c.dbg = {}
    for name, shape, dt in dbg:
        c.dbg[name] = nc.dram_tensor("dbg_" + name, list(shape), dt, kind="ExternalOutput")

    c.u_spill = nc.dram_tensor("u_spill", [NDT, 128, T], BF16)
    c.sg_spill = nc.dram_tensor("sg_spill", [NDT, 128, T], BF16)
    c.xn_spill = nc.dram_tensor("xn_spill", [NKH, 128, T], BF16)
    c.stats_scr = nc.dram_tensor("stats_scr", [4, T], F32)
    c.bc_scr = nc.dram_tensor("bc_scr", [32, T], BF16)
    c.qT_cache = nc.dram_tensor("qT_cache", [NKH, NKH, 128, 128], BF16)
    c.mixer_mine = nc.dram_tensor("mixer_mine", [2, NKH, 128, TCH], F32)
    c.hidden_spill = nc.dram_tensor("hidden_spill", [2, NKH, 128, TCH], F32)
    c.v_spill = nc.dram_tensor("v_spill", [NMF, 128, TCH], BF16)
    c.g_spill = nc.dram_tensor("g_spill", [NKH, 128, TCH], BF16)
    c.w_xbf = nc.dram_tensor("w_xbf", [NDT, 128, 160], BF16)
    c.w_inbf = nc.dram_tensor("w_inbf", [2 * NDT, NKH, 128, 128], BF16)
    c.woutbf = nc.dram_tensor("woutbf", [NKH, NDT, 128, 128], BF16)
    c.wgbf = nc.dram_tensor("wgbf", [NMF, NKH, 128, 128], BF16)
    c.wubf = nc.dram_tensor("wubf", [NMF, NKH, 128, 128], BF16)
    c.wdnbf = nc.dram_tensor("wdnbf", [NKH, NMF, 128, 128], BF16)

    c.sh_ssm = [nc.dram_tensor("sh_ssmE", [160, T], F32, addr_space="Shared"),
                nc.dram_tensor("sh_ssmO", [160, T], F32, addr_space="Shared")]
    c.sh_mix = [nc.dram_tensor("sh_mixE", [2, NKH, 128, TCH], F32, addr_space="Shared"),
                nc.dram_tensor("sh_mixO", [2, NKH, 128, TCH], F32, addr_space="Shared")]
    c.fl = [nc.dram_tensor("fl_E", [1, 16], I32, addr_space="Shared"),
            nc.dram_tensor("fl_O", [1, 16], I32, addr_space="Shared")]

    from contextlib import ExitStack
    with ExitStack() as ctx:
        en = ctx.enter_context

        c.ar1 = en(nc.sbuf_tensor("ar1", [128, 32768], BF16))
        c.ar2 = en(nc.sbuf_tensor("ar2", [128, 24576], BF16))
        c.af = en(nc.sbuf_tensor("af", [128, 5120], F32))
        c.ssm1 = en(nc.sbuf_tensor("ssm1", [128, T], F32))
        c.ssmA = en(nc.sbuf_tensor("ssmA", [128, T], F32))
        c.s_trn = en(nc.sbuf_tensor("s_trn", [128, T], BF16))
        c.s_cst = en(nc.sbuf_tensor("s_cst", [128, 64], F32))
        c.s_alogA = en(nc.sbuf_tensor("s_alogA", [128, 256], F32))
        c.s_convw = en(nc.sbuf_tensor("s_convw", [128, 64], F32))
        c.s_wdtT = en(nc.sbuf_tensor("s_wdtT", [128, NDT * 128], BF16))
        c.s_upre = en(nc.sbuf_tensor("s_upre", [128, 515], F32))
        c.s_stash = en(nc.sbuf_tensor("s_stash", [128, 48], F32))
        c.s_hstate = en(nc.sbuf_tensor("s_hstate", [128, 256], BF16))
        c.s_ident = en(nc.sbuf_tensor("s_ident", [128, 128], BF16))
        c.s_onesc = en(nc.sbuf_tensor("s_onesc", [128, 1], BF16))
        c.s_inln = en(nc.sbuf_tensor("s_inln", [1, H], BF16))
        c.s_preln = en(nc.sbuf_tensor("s_preln", [1, H], BF16))
        c.s_dtln = en(nc.sbuf_tensor("s_dtln", [1, 128], BF16))
        c.s_bcsel = en(nc.sbuf_tensor("s_bcsel", [2, 32], BF16))
        c.s_grpsel = en(nc.sbuf_tensor("s_grpsel", [32, 2], BF16))
        c.s_rowt = en(nc.sbuf_tensor("s_rowt", [2, 1024], BF16))
        c.s_rowf = en(nc.sbuf_tensor("s_rowf", [1, 2048], F32))
        c.s_row2 = en(nc.sbuf_tensor("s_row2", [2, 512], F32))
        c.s_bcn = en(nc.sbuf_tensor("s_bcn", [32, 512], BF16))
        c.s_sq2 = en(nc.sbuf_tensor("s_sq2", [32, 512], BF16))
        c.s_flags = en(nc.sbuf_tensor("s_flags", [1, 16], I32))
        c.s_flagrd = en(nc.sbuf_tensor("s_flagrd", [1, 1], I32))
        c.s_role = en(nc.sbuf_tensor("s_role", [1, 1], I32))
        c.s_nonce = en(nc.sbuf_tensor("s_nonce", [1, 1], I32))
        c.s_kc = en(nc.sbuf_tensor("s_kc", [128, 8], F32))
        nc.const_aps.register(F32, 1e-5, c.s_kc[:, 0:1])
        nc.const_aps.register(F32, 1e-6, c.s_kc[:, 1:2])
        nc.const_aps.register(F32, 1.0, c.s_kc[:, 2:3])

        c.p = [en(nc.psum_tensor(f"p{i}", [128, 512], F32)) for i in range(6)]
        c.pT = en(nc.psum_tensor("pT", [128, 128], BF16))
        c.pst = en(nc.psum_tensor("pst", [1, 512], F32))

        c.qs = Ctr(en(nc.semaphore("qs")), 16)
        c.qg = Ctr(en(nc.semaphore("qg")), 16)
        c.qa = Ctr(en(nc.semaphore("qa")), 16)
        c.cv = Ctr(en(nc.semaphore("cv")))
        c.cs = Ctr(en(nc.semaphore("cs")))
        c.ct = Ctr(en(nc.semaphore("ct")))
        c.cp = Ctr(en(nc.semaphore("cp")))
        c.cy = Ctr(en(nc.semaphore("cy")))
        c.qv = [Ctr(en(nc.semaphore(f"qv{i}")), 16) for i in range(4)]
        c.wsem = [en(nc.semaphore(f"wc{i}")) for i in range(6)]
        c.pollsems = [en(nc.semaphore(f"pl{i}")) for i in range(4)]

        c.block = en(nc.Block())

        with nc.named_scope("phase0"):
            emit_phase0(c)
        with nc.named_scope("ternary"):
            emit_ternary(c)
        with nc.named_scope("phaseA"):
            emit_phaseA(c)
        if stage >= 2:
            with nc.named_scope("exch_ssm"):
                emit_exchange_ssm(c)
            with nc.named_scope("bprep"):
                emit_bprep(c)
        if stage >= 3:
            with nc.named_scope("scan"):
                emit_scan(c)
        if stage >= 4:
            with nc.named_scope("phaseC"):
                emit_phaseC(c)
        emit_dbg(c)
        emit_tail(c)

    return nc


def emit_poll(c, sy, sem, flag_tensor, word, tag):
    with sy.register(f"pf{tag}") as f, sy.register(f"pi{tag}") as it, \
            sy.register(f"pn{tag}") as nv:
        sy.reg_load(nv, c.s_nonce[0:1, 0:1])
        if word:
            sy.reg_add(nv, nv, word)
        sy.reg_mov(f, 0)
        sy.reg_mov(it, 0)
        with sy.While(s_logical_and(make_scalar_value(f) != make_scalar_value(nv),
                                    make_scalar_value(it) < POLL_CAP)):
            sy.dma_start(out=c.s_flagrd[:, :],
                         in_=flag_tensor[0:1, word:word + 1]).then_inc(sem, 16)
            sy.reg_add(it, it, 1)
            sy.wait_ge(sem, make_scalar_value(it) * 16)
            sy.reg_load(f, c.s_flagrd[0:1, 0:1])


def emit_flag(c, sy, word):
    c.flagseq = getattr(c, "flagseq", 0) + 1
    with sy.register(f"fw{word}_{c.flagseq}") as nv:
        sy.reg_load(nv, c.s_nonce[0:1, 0:1])
        if word:
            sy.reg_add(nv, nv, word)
        sy.reg_save(c.s_flags[0:1, word:word + 1], nv)
    with sy.register(f"fr{word}_{c.flagseq}") as role:
        sy.reg_load(role, c.s_role[0:1, 0:1])
        with sy.If(make_scalar_value(role) == 0):
            sy.dma_start(out=c.fl[0][0:1, :], in_=c.s_flags[0:1, :]).then_inc(c.qs.sem, 16)
        with sy.Else():
            sy.dma_start(out=c.fl[1][0:1, :], in_=c.s_flags[0:1, :]).then_inc(c.qs.sem, 16)
    c.qs.n += 16
    sy.wait_ge(c.qs.sem, c.qs.n)


def emit_phase0(c):
    block = c.block
    qs, qg, cv, cs = c.qs, c.qg, c.cv, c.cs

    @block.sync
    def _(sy):
        sy.dma_start(out=c.s_role[:, :], in_=c.role_h[:, :]).then_inc(qs.sem, 16)
        sy.dma_start(out=c.s_nonce[:, :], in_=c.nonce_h[:, :]).then_inc(qs.sem, 16)
        sy.dma_start(out=c.s_cst[:, 0:16], in_=c.convb[:, :]).then_inc(qs.sem, 16)
        sy.dma_start(out=c.s_cst[:, 16:32], in_=c.bdt[:, :]).then_inc(qs.sem, 16)
        sy.dma_start(out=c.s_cst[:, 32:48], in_=c.dskip[:, :]).then_inc(qs.sem, 16)
        sy.dma_start(out=c.s_cst[:, 48:64], in_=c.ffnb[:, :]).then_inc(qs.sem, 16)
        sy.dma_start(out=c.s_alogA[:, :], in_=c.alog[:, :]).then_inc(qs.sem, 16)
        sy.dma_start(out=c.s_convw.ap().rearrange("p (d k) -> p d k", d=NDT),
                     in_=c.convw.ap().rearrange("d p k -> p d k")).then_inc(qs.sem, 16)
    qs.n += 8 * 16

    @block.gpsimd
    def _(g):
        g = EW(g, c.cp)
        g.dma_start(out=c.s_inln[:, :], in_=c.inln_row[:, :]).then_inc(qg.sem, 16)
        g.dma_start(out=c.s_preln[:, :], in_=c.preln_row[:, :]).then_inc(qg.sem, 16)
        g.dma_start(out=c.s_dtln[:, :], in_=c.dtln_row[:, :]).then_inc(qg.sem, 16)
        g.dma_start(out=c.s_bcsel[:, :], in_=c.bc_sel[:, :]).then_inc(qg.sem, 16)
        g.dma_start(out=c.s_grpsel[:, :], in_=c.grp_sel[:, :]).then_inc(qg.sem, 16)
        g.dma_start(out=c.s_ident[:, :], in_=c.ident[:, :]).then_inc(qg.sem, 16)
        g.dma_start(out=c.s_wdtT.ap().rearrange("p (d q) -> p d q", d=NDT),
                    in_=c.w_dtT.ap().rearrange("d p q -> p d q")).then_inc(qg.sem, 16)
    qg.n += 7 * 16

    @block.gpsimd
    def _(g):
        g.dma_start(out=c.w_xbf.ap().rearrange("d p q -> d (p q)"),
                    in_=c.w_xT.ap().rearrange("d p q -> d (p q)")).then_inc(c.wsem[0], 16)
        g.dma_start(out=c.w_inbf.ap().rearrange("d k p q -> (d k) (p q)"),
                    in_=c.w_inT.ap().rearrange("d k p q -> (d k) (p q)")).then_inc(c.wsem[1], 16)
        g.dma_start(out=c.woutbf.ap().rearrange("m k p q -> (m k) (p q)"),
                    in_=c.w_outT.ap().rearrange("m k p q -> (m k) (p q)")).then_inc(c.wsem[2], 16)
        g.dma_start(out=c.wgbf.ap().rearrange("m k p q -> (m k) (p q)"),
                    in_=c.w_gT.ap().rearrange("m k p q -> (m k) (p q)")).then_inc(c.wsem[3], 16)
        g.dma_start(out=c.wubf.ap().rearrange("m k p q -> (m k) (p q)"),
                    in_=c.w_uT.ap().rearrange("m k p q -> (m k) (p q)")).then_inc(c.wsem[4], 16)
        g.dma_start(out=c.wdnbf.ap().rearrange("m k p q -> (m k) (p q)"),
                    in_=c.w_dnT.ap().rearrange("m k p q -> (m k) (p q)")).then_inc(c.wsem[5], 16)

    @block.vector
    def _(v):
        v = EW(v, c.cv)
        v.memset(c.s_onesc[:, :], 1.0)
        v.memset(c.s_hstate[:, :], 0.0)
        v.memset(c.s_kc[:, 0:1], 1e-5)
        v.memset(c.s_kc[:, 1:2], 1e-6)
        v.memset(c.s_kc[:, 2:3], 1.0)
        v.memset(c.s_stash[:, :], 0.0)

    @block.scalar
    def _(sc):
        sc = EW(sc, c.cs)
        sc.wait_ge(qs.sem, qs.n)
        sc.activation(c.s_alogA[:, :], c.s_alogA[:, :], AF.Exp)

    @block.vector
    def _(v):
        v = EW(v, c.cv)
        v.wait_ge(cs.sem, cs.n)
        v.tensor_scalar_mul(c.s_alogA[:, :], c.s_alogA[:, :], -1.0)

    @block.sync
    def _(sy):
        sy.wait_ge(qs.sem, qs.n)
        with sy.register("nv0") as nv0:
            sy.reg_load(nv0, c.s_nonce[0:1, 0:1])
            sy.reg_sub(nv0, nv0, 1)
            for i in range(16):
                sy.reg_save(c.s_flags[0:1, i:i + 1], nv0)


def emit_ternary(c):
    block = c.block
    qs, qa, qg, cv, cs, ct = c.qs, c.qa, c.qg, c.cv, c.cs, c.ct

    for m in range(NKH):
        @block.scalar
        def _(sc, m=m):
            sc = EW(sc, c.cs)
            sc.wait_ge(cv.sem, cv.n)
            sc.wait_ge(qs.sem, qs.n)
            sc.dma_start(out=c.af[:, 0:2048], in_=c.w_tern[m * 128:(m + 1) * 128, :]).then_inc(qa.sem, 16)
        qa.n += 16

        @block.vector
        def _(v, m=m):
            v = EW(v, c.cv)
            v.wait_ge(qa.sem, qa.n)
            v.reduce_sum(c.af[:, 2048:2049], c.af[:, 0:2048], AX.X, apply_absolute_value=True)
            v.tensor_scalar_mul(c.af[:, 2048:2049], c.af[:, 2048:2049], 0.7 / H)
            v.tensor_scalar_mul(c.af[:, 2049:2050], c.af[:, 2048:2049], -1.0)
            v.tensor_single_scalar(c.ar2[:, 4096:6144], c.af[:, 0:2048], c.af[:, 2048:2049], ALU.is_gt)
            v.tensor_single_scalar(c.ar2[:, 6144:8192], c.af[:, 0:2048], c.af[:, 2049:2050], ALU.is_lt)
            v.tensor_tensor(c.ar2[:, 0:2048], c.ar2[:, 4096:6144], c.ar2[:, 6144:8192], ALU.subtract)
            v.nop()

        for k in range(NKH):
            @block.tensor
            def _(t, m=m, k=k):
                t.wait_ge(cv.sem, cv.n)
                t.wait_ge(qg.sem, c.qg.n)
                t.transpose(c.pT[:, :], c.ar2[:, k * 128:(k + 1) * 128], c.s_ident[:, :]).then_inc(ct.sem, 1)
            ct.n += 1

            @block.vector
            def _(v, m=m, k=k):
                v = EW(v, c.cv)
                v.wait_ge(ct.sem, ct.n)
                v.wait_ge(qs.sem, qs.n)
                v.tensor_copy(c.ar2[:, 2048 + (k % 2) * 128:2048 + (k % 2) * 128 + 128], c.pT[:, :])
                v.nop()

            @block.sync
            def _(sy, m=m, k=k):
                sy.wait_ge(cv.sem, cv.n)
                sy.dma_start(out=c.qT_cache[k, m, :, :],
                             in_=c.ar2[:, 2048 + (k % 2) * 128:2048 + (k % 2) * 128 + 128]).then_inc(qs.sem, 16)
            qs.n += 16


def emit_phaseA(c):
    block = c.block
    qs, qg, qa, cv, cs, ct = c.qs, c.qg, c.qa, c.cv, c.cs, c.ct

    for st_ in range(16):
        col0 = st_ * 256
        for j in range(NKH):
            @block.scalar
            def _(sc, st_=st_, j=j, col0=col0):
                sc = EW(sc, c.cs)
                sc.wait_ge(cv.sem, cv.n)
                sc.dma_start(out=c.af[:, 0:256],
                             in_=c.xT[j * 128:(j + 1) * 128, col0:col0 + 256]).then_inc(qa.sem, 16)
            qa.n += 16

            @block.vector
            def _(v, st_=st_, j=j):
                v = EW(v, c.cv)
                v.wait_ge(qa.sem, qa.n)
                if j > 0:
                    v.wait_ge(ct.sem, ct.n)
                v.tensor_copy(c.ar2[:, 0:256], c.af[:, 0:256])
                v.tensor_tensor(c.ar2[:, 256:512], c.ar2[:, 0:256], c.ar2[:, 0:256], ALU.mult)
                v.nop()

            @block.tensor
            def _(t, st_=st_, j=j):
                t.wait_ge(cv.sem, cv.n)
                t.matmul(c.pst[0:1, 0:256], c.s_onesc[:, :], c.ar2[:, 0:256],
                         start=(j == 0), stop=(j == NKH - 1))
                t.matmul(c.p[5][0:1, 0:256], c.s_onesc[:, :], c.ar2[:, 256:512],
                         start=(j == 0), stop=(j == NKH - 1)).then_inc(ct.sem, 1)
            ct.n += 1

        @block.vector
        def _(v, st_=st_):
            v = EW(v, c.cv)
            v.wait_ge(ct.sem, ct.n)
            v.wait_ge(qs.sem, qs.n)
            v.tensor_scalar_mul(c.s_rowf[0:1, 0:256], c.pst[0:1, 0:256], 1.0 / H)
            v.tensor_scalar_mul(c.s_rowf[0:1, 512:768], c.p[5][0:1, 0:256], 1.0 / H)
            v.tensor_tensor(c.s_rowf[0:1, 1024:1280], c.s_rowf[0:1, 0:256], c.s_rowf[0:1, 0:256], ALU.mult)
            v.tensor_tensor(c.s_rowf[0:1, 512:768], c.s_rowf[0:1, 512:768], c.s_rowf[0:1, 1024:1280], ALU.subtract)
            v.nop()

        @block.scalar
        def _(sc, st_=st_):
            sc = EW(sc, c.cs)
            sc.wait_ge(cv.sem, cv.n)
            sc.activation(c.s_rowf[0:1, 1024:1280], c.s_rowf[0:1, 512:768], AF.Ln, bias=1e-5)
            sc.activation(c.s_rowf[0:1, 512:768], c.s_rowf[0:1, 1024:1280], AF.Exp, scale=-0.5)
            sc.nop()

        @block.vector
        def _(v, st_=st_):
            v = EW(v, c.cv)
            v.wait_ge(cs.sem, cs.n)
            v.tensor_tensor(c.s_rowf[0:1, 1024:1280], c.s_rowf[0:1, 0:256], c.s_rowf[0:1, 512:768], ALU.mult)
            v.tensor_scalar_mul(c.s_rowf[0:1, 1024:1280], c.s_rowf[0:1, 1024:1280], -1.0)
            v.nop()

        @block.sync
        def _(sy, st_=st_, col0=col0):
            sy.wait_ge(cv.sem, cv.n)
            sy.dma_start(out=c.stats_scr[0:1, col0:col0 + 256], in_=c.s_rowf[0:1, 512:768]).then_inc(qs.sem, 16)
            sy.dma_start(out=c.stats_scr[1:2, col0:col0 + 256], in_=c.s_rowf[0:1, 1024:1280]).then_inc(qs.sem, 16)
        qs.n += 32

    for st_ in range(8):
        col0 = st_ * 512

        @block.sync
        def _(sy, st_=st_, col0=col0):
            sy.wait_ge(qs.sem, qs.n)
            sy.wait_ge(cv.sem, cv.n)
            sy.dma_start(out=c.af[:, 512:1024],
                         in_=bass.AP(c.stats_scr, 0 * T + col0, [[0, 128], [1, 512]])).then_inc(qs.sem, 16)
            sy.dma_start(out=c.af[:, 1024:1536],
                         in_=bass.AP(c.stats_scr, 1 * T + col0, [[0, 128], [1, 512]])).then_inc(qs.sem, 16)
        qs.n += 32

        for j in range(NKH):
            @block.scalar
            def _(sc, st_=st_, j=j, col0=col0):
                sc = EW(sc, c.cs)
                sc.wait_ge(cv.sem, cv.n)
                sc.dma_start(out=c.af[:, 0:512],
                             in_=c.xT[j * 128:(j + 1) * 128, col0:col0 + 512]).then_inc(qa.sem, 16)
            qa.n += 16

            @block.vector
            def _(v, st_=st_, j=j):
                v = EW(v, c.cv)
                v.wait_ge(qa.sem, qa.n)
                v.wait_ge(qs.sem, qs.n)
                if j > 0:
                    v.wait_ge(ct.sem, ct.n)
                v.tensor_tensor(c.af[:, 1536:2048], c.af[:, 0:512], c.af[:, 512:1024], ALU.mult)
                v.tensor_tensor(c.ar2[:, 0:512], c.af[:, 1536:2048], c.af[:, 1024:1536], ALU.add)
                v.tensor_tensor(c.ar2[:, 512:1024], c.ar2[:, 0:512], c.ar2[:, 0:512], ALU.mult)
                v.nop()

            @block.sync
            def _(sy, st_=st_, j=j, col0=col0):
                sy.wait_ge(cv.sem, cv.n)
                sy.dma_start(out=c.xn_spill[j, :, col0:col0 + 512], in_=c.ar2[:, 0:512]).then_inc(qs.sem, 16)
            qs.n += 16

            @block.tensor
            def _(t, st_=st_, j=j):
                t.wait_ge(cv.sem, cv.n)
                t.matmul(c.pst[0:1, :], c.s_onesc[:, :], c.ar2[:, 512:1024],
                         start=(j == 0), stop=(j == NKH - 1)).then_inc(ct.sem, 1)
            ct.n += 1

        @block.scalar
        def _(sc, st_=st_, col0=col0):
            sc = EW(sc, c.cs)
            sc.wait_ge(ct.sem, ct.n)
            sc.activation(c.s_rowf[0:1, 0:512], c.pst[0:1, :], AF.Ln, bias=1e-6, scale=1.0 / H)
            sc.activation(c.s_rowf[0:1, 0:512], c.s_rowf[0:1, 0:512], AF.Exp, scale=-0.5)
            sc.nop()

        @block.sync
        def _(sy, st_=st_, col0=col0):
            sy.wait_ge(cs.sem, cs.n)
            sy.dma_start(out=c.stats_scr[2:3, col0:col0 + 512], in_=c.s_rowf[0:1, 0:512]).then_inc(qs.sem, 16)
        qs.n += 16

    c.wbuf_use = [ct.n, ct.n]
    for th in range(2):
        th0 = th * 2048
        for j in range(NKH):
            @block.sync
            def _(sy, th=th, j=j, th0=th0):
                sy.wait_ge(cv.sem, cv.n)
                sy.wait_ge(qs.sem, qs.n)
                sy.dma_start(out=c.ar2[:, 0:2048], in_=c.xn_spill[j, :, th0:th0 + 2048]).then_inc(qs.sem, 16)
            qs.n += 16

            for tq in range(4):
                @block.sync
                def _(sy, th=th, j=j, tq=tq, th0=th0):
                    if j == 0 or True:
                        sy.wait_ge(cv.sem, cv.n)
                        sy.dma_start(out=c.s_rowf[0:1, 1536:2048],
                                     in_=c.stats_scr[2:3, th0 + tq * 512:th0 + (tq + 1) * 512]).then_inc(qs.sem, 16)
                qs.n += 16

                @block.vector
                def _(v, th=th, j=j, tq=tq):
                    v = EW(v, c.cv)
                    v.wait_ge(qs.sem, qs.n)
                    v.wait_ge(ct.sem, ct.n)
                    v.tensor_copy(c.s_rowt[0:1, 0:512], c.s_rowf[0:1, 1536:2048])
                    v.nop()

                @block.tensor
                def _(t, th=th, j=j, tq=tq):
                    t.wait_ge(cv.sem, cv.n)
                    t.wait_ge(qg.sem, qg.n)
                    t.matmul(c.p[2][:, :], c.s_inln[0:1, j * 128:(j + 1) * 128],
                             c.s_rowt[0:1, 0:512], start=True, stop=True).then_inc(ct.sem, 1)
                ct.n += 1

                @block.vector
                def _(v, th=th, j=j, tq=tq):
                    v = EW(v, c.cv)
                    v.wait_ge(ct.sem, ct.n)
                    v.tensor_tensor(c.ar1[:, j * 2048 + tq * 512:j * 2048 + (tq + 1) * 512],
                                    c.ar2[:, tq * 512:(tq + 1) * 512], c.p[2][:, :], ALU.mult)
                    v.nop()

        for dt_ in range(32):
            buf = dt_ % 2
            wb = buf * 2048

            @block.scalar
            def _(sc, th=th, dt_=dt_, buf=buf, wb=wb):
                sc.wait_ge(ct.sem, c.wbuf_use[buf])
                if dt_ <= 1:
                    if th == 0:
                        sc.wait_ge(c.wsem[1], 16)
                    sc.wait_ge(cv.sem, cv.n)
                    sc.wait_ge(qs.sem, qs.n)
                sc.dma_start(out=c.ar2[:, wb:wb + 2048].rearrange("p (k q) -> p k q", k=NKH),
                             in_=c.w_inbf.ap()[dt_].rearrange("k p q -> p k q")).then_inc(qa.sem, 16)
            qa.n += 16

            if not hasattr(c, "p2banks"):
                c.p2banks = [c.p[0], c.p[1], c.p[2], c.p[5]]
                c.p2buse = [(cs.sem, cs.n)] * 4
            for tq in range(4):
                bank = (dt_ * 4 + tq) % 4
                pb = c.p2banks[bank]

                @block.tensor
                def _(t, th=th, dt_=dt_, tq=tq, wb=wb, th0=th0, pb=pb,
                      pfree=c.p2buse[bank]):
                    t.wait_ge(qa.sem, qa.n)
                    if dt_ == 0:
                        t.wait_ge(cv.sem, cv.n)
                    t.wait_ge(pfree[0], pfree[1])
                    last = None
                    for k in range(NKH):
                        last = t.matmul(pb[:, :], c.ar2[:, wb + k * 128:wb + (k + 1) * 128],
                                        c.ar1[:, k * 2048 + tq * 512:k * 2048 + (tq + 1) * 512],
                                        start=(k == 0), stop=(k == NKH - 1))
                    last.then_inc(ct.sem, 1)
                ct.n += 1
                emit_phaseA_evac(c, th, dt_, tq, pb, bank)
            c.wbuf_use[buf] = ct.n


def emit_phaseA_evac(c, th, dt_, tq, pb, bank):
    block = c.block
    qs, qg, cv, cs, ct = c.qs, c.qg, c.cv, c.cs, c.ct
    col = th * 2048 + tq * 512
    is_u = dt_ < NDT

    if is_u:
        @block.vector
        def _(v):
            v = EW(v, c.cv)
            v.wait_ge(ct.sem, ct.n)
            v.tensor_copy(c.s_upre[:, 3:515], pb[:, :])
            if th == 0 and tq == 0:
                v.memset(c.s_upre[:, 0:3], 0.0)
            else:
                v.tensor_copy(c.s_upre[:, 0:3], c.s_stash[:, dt_ * 3:dt_ * 3 + 3])
            v.tensor_single_scalar(c.af[:, 4608:5120], c.s_upre[:, 0:512],
                                   c.s_convw[:, dt_ * 4:dt_ * 4 + 1], ALU.mult)
            for k in range(1, 4):
                v.scalar_tensor_tensor(c.af[:, 4608:5120], c.s_upre[:, k:k + 512],
                                       c.s_convw[:, dt_ * 4 + k:dt_ * 4 + k + 1],
                                       c.af[:, 4608:5120], ALU.mult, ALU.add)
            v.tensor_copy(c.s_stash[:, dt_ * 3:dt_ * 3 + 3], c.s_upre[:, 512:515])
            v.nop()
        c.p2buse[bank] = (c.cv.sem, c.cv.n)

        @block.scalar
        def _(sc):
            sc = EW(sc, c.cs)
            sc.wait_ge(cv.sem, cv.n)
            sc.activation(c.af[:, 4096:4608], c.af[:, 4608:5120], AF.Silu,
                          bias=c.s_cst[:, dt_:dt_ + 1])
            sc.nop()

        @block.vector
        def _(v):
            v = EW(v, c.cv)
            v.wait_ge(cs.sem, cs.n)
            v.wait_ge(qs.sem, qs.n)
            v.wait_ge(ct.sem, ct.n)
            v.tensor_copy(c.ar2[:, 4096:4608], c.af[:, 4096:4608])
            v.nop()

        @block.sync
        def _(sy):
            sy.wait_ge(cv.sem, cv.n)
            sy.dma_start(out=c.u_spill[dt_, :, col:col + 512], in_=c.ar2[:, 4096:4608]).then_inc(qs.sem, 16)
        qs.n += 16

        if tq == 0:
            @block.scalar
            def _(sc):
                sc.wait_ge(ct.sem, ct.n)
                if th == 0 and dt_ == 0:
                    sc.wait_ge(c.wsem[0], 16)
                sc.dma_start(out=c.ar2[:, 4608:4768], in_=c.w_xbf.ap()[dt_]).then_inc(c.qa.sem, 16)
            c.qa.n += 16

        @block.tensor
        def _(t):
            t.wait_ge(c.qa.sem, c.qa.n)
            t.wait_ge(cv.sem, cv.n)
            t.matmul(c.p[3][:, :], c.ar2[:, 4608:4736], c.ar2[:, 4096:4608], start=True, stop=True)
            t.matmul(c.p[4][0:32, :], c.ar2[:, 4736:4768], c.ar2[:, 4096:4608], start=True, stop=True).then_inc(ct.sem, 1)
        ct.n += 1

        @block.vector
        def _(v):
            v = EW(v, c.cv)
            v.wait_ge(ct.sem, ct.n)
            sl = slice(col, col + 512)
            if dt_ == 0:
                v.tensor_copy(c.ssm1[:, sl], c.p[3][:, :])
                v.tensor_copy(c.ssmA[0:32, sl], c.p[4][0:32, :])
            else:
                v.tensor_tensor(c.ssm1[:, sl], c.ssm1[:, sl], c.p[3][:, :], ALU.add)
                v.tensor_tensor(c.ssmA[0:32, sl], c.ssmA[0:32, sl], c.p[4][0:32, :], ALU.add)
            v.nop()
    else:
        @block.scalar
        def _(sc):
            sc = EW(sc, c.cs)
            sc.wait_ge(ct.sem, ct.n)
            sc.wait_ge(cv.sem, cv.n)
            sc.activation(c.af[:, 4096:4608], pb[:, :], AF.Silu)
            sc.nop()
        c.p2buse[bank] = (c.cs.sem, c.cs.n)

        @block.vector
        def _(v):
            v = EW(v, c.cv)
            v.wait_ge(cs.sem, cs.n)
            v.wait_ge(qs.sem, qs.n)
            v.tensor_copy(c.ar2[:, 4096:4608], c.af[:, 4096:4608])
            v.nop()

        @block.sync
        def _(sy):
            sy.wait_ge(cv.sem, cv.n)
            sy.dma_start(out=c.sg_spill[dt_ - NDT, :, col:col + 512], in_=c.ar2[:, 4096:4608]).then_inc(qs.sem, 16)
        qs.n += 16


def emit_exchange_ssm(c):
    if c.pairless:
        return
    block = c.block
    qs, cv = c.qs, c.cv

    @block.sync
    def _(sy):
        sy.wait_ge(cv.sem, cv.n)
        with sy.register("exr") as role:
            sy.reg_load(role, c.s_role[0:1, 0:1])
            with sy.If(make_scalar_value(role) == 0):
                sy.dma_start(out=c.sh_ssm[0][0:128, :], in_=c.ssm1[:, :]).then_inc(qs.sem, 16)
                sy.dma_start(out=c.sh_ssm[0][128:160, :], in_=c.ssmA[0:32, :]).then_inc(qs.sem, 16)
            with sy.Else():
                sy.dma_start(out=c.sh_ssm[1][0:128, :], in_=c.ssm1[:, :]).then_inc(qs.sem, 16)
                sy.dma_start(out=c.sh_ssm[1][128:160, :], in_=c.ssmA[0:32, :]).then_inc(qs.sem, 16)
        qs.n += 32
        sy.wait_ge(qs.sem, qs.n)
        emit_flag(c, sy, 0)
        with sy.register("exr2") as role2:
            sy.reg_load(role2, c.s_role[0:1, 0:1])
            with sy.If(make_scalar_value(role2) == 0):
                emit_poll(c, sy, c.pollsems[0], c.fl[1], 0, "sA")
            with sy.Else():
                emit_poll(c, sy, c.pollsems[0], c.fl[0], 0, "sB")

    for pc in range(4):
        cl = pc * 1024

        @block.sync
        def _(sy, pc=pc, cl=cl):
            if pc > 0:
                sy.wait_ge(cv.sem, cv.n)
            with sy.register(f"exp{pc}") as role:
                sy.reg_load(role, c.s_role[0:1, 0:1])
                with sy.If(make_scalar_value(role) == 0):
                    sy.dma_start(out=c.af[:, 0:1024], in_=c.sh_ssm[1][0:128, cl:cl + 1024]).then_inc(qs.sem, 16)
                    sy.dma_start(out=c.af[0:32, 1024:2048], in_=c.sh_ssm[1][128:160, cl:cl + 1024]).then_inc(qs.sem, 16)
                with sy.Else():
                    sy.dma_start(out=c.af[:, 0:1024], in_=c.sh_ssm[0][0:128, cl:cl + 1024]).then_inc(qs.sem, 16)
                    sy.dma_start(out=c.af[0:32, 1024:2048], in_=c.sh_ssm[0][128:160, cl:cl + 1024]).then_inc(qs.sem, 16)
            qs.n += 32
            sy.wait_ge(qs.sem, qs.n)
            sy.nop().then_inc(c.cy.sem, 1)
        c.cy.n += 1

        @block.vector
        def _(v, pc=pc, cl=cl):
            v = EW(v, c.cv)
            v.wait_ge(c.cy.sem, c.cy.n)
            v.tensor_tensor(c.ssm1[:, cl:cl + 1024], c.ssm1[:, cl:cl + 1024], c.af[:, 0:1024], ALU.add)
            v.tensor_tensor(c.ssmA[0:32, cl:cl + 1024], c.ssmA[0:32, cl:cl + 1024],
                            c.af[0:32, 1024:2048], ALU.add)


def emit_bprep(c):
    block = c.block
    qs, cv, cs, ct = c.qs, c.cv, c.cs, c.ct

    for st_ in range(8):
        sl = slice(st_ * 512, (st_ + 1) * 512)

        @block.vector
        def _(v, sl=sl):
            v = EW(v, c.cv)
            v.wait_ge(ct.sem, ct.n)
            v.tensor_copy(c.s_trn[:, sl], c.ssm1[:, sl])
            v.tensor_tensor(c.ar2[:, 0:512], c.s_trn[:, sl], c.s_trn[:, sl], ALU.mult)

        @block.tensor
        def _(t, sl=sl):
            t.wait_ge(cv.sem, cv.n)
            t.wait_ge(cs.sem, cs.n)
            t.matmul(c.pst[0:1, :], c.s_onesc[:, :], c.ar2[:, 0:512],
                     start=True, stop=True).then_inc(ct.sem, 1)
        ct.n += 1

        @block.scalar
        def _(sc, sl=sl):
            sc = EW(sc, c.cs)
            sc.wait_ge(ct.sem, ct.n)
            sc.wait_ge(cv.sem, cv.n)
            sc.activation(c.s_rowf[0:1, 0:512], c.pst[0:1, :], AF.Ln, bias=1e-6, scale=1.0 / 128)
            sc.activation(c.s_rowf[0:1, 0:512], c.s_rowf[0:1, 0:512], AF.Exp, scale=-0.5)

        @block.vector
        def _(v, sl=sl):
            v = EW(v, c.cv)
            v.wait_ge(cs.sem, cs.n)
            v.tensor_copy(c.s_rowt[0:1, 0:512], c.s_rowf[0:1, 0:512])

        @block.tensor
        def _(t, sl=sl):
            t.wait_ge(cv.sem, cv.n)
            t.matmul(c.p[2][:, :], c.s_dtln[0:1, :], c.s_rowt[0:1, 0:512],
                     start=True, stop=True).then_inc(ct.sem, 1)
        ct.n += 1

        @block.vector
        def _(v, sl=sl):
            v = EW(v, c.cv)
            v.wait_ge(ct.sem, ct.n)
            v.tensor_tensor(c.s_trn[:, sl], c.ssm1[:, sl], c.p[2][:, :], ALU.mult)

        @block.vector
        def _(v, sl=sl):
            v = EW(v, c.cv)
            v.tensor_copy(c.s_sq2[0:32, :], c.ssmA[0:32, sl])
            v.tensor_tensor(c.s_sq2[0:32, :], c.s_sq2[0:32, :], c.s_sq2[0:32, :], ALU.mult)

        @block.tensor
        def _(t, sl=sl):
            t.wait_ge(cv.sem, cv.n)
            t.matmul(c.p[2][0:2, :], c.s_grpsel[:, :], c.s_sq2[0:32, :],
                     start=True, stop=True).then_inc(ct.sem, 1)
        ct.n += 1

        @block.scalar
        def _(sc, sl=sl):
            sc = EW(sc, c.cs)
            sc.wait_ge(ct.sem, ct.n)
            sc.activation(c.s_row2[0:2, :], c.p[2][0:2, :], AF.Ln, bias=1e-6, scale=1.0 / 16)
            sc.activation(c.s_row2[0:2, :], c.s_row2[0:2, :], AF.Exp, scale=-0.5)

        @block.vector
        def _(v, sl=sl):
            v = EW(v, c.cv)
            v.wait_ge(cs.sem, cs.n)
            v.tensor_copy(c.s_rowt[0:2, 0:512], c.s_row2[0:2, :])

        @block.tensor
        def _(t, sl=sl):
            t.wait_ge(cv.sem, cv.n)
            t.matmul(c.p[2][0:32, :], c.s_bcsel[:, :], c.s_rowt[0:2, 0:512],
                     start=True, stop=True).then_inc(ct.sem, 1)
        ct.n += 1

        @block.vector
        def _(v, sl=sl):
            v = EW(v, c.cv)
            v.wait_ge(ct.sem, ct.n)
            v.wait_ge(qs.sem, qs.n)
            v.tensor_tensor(c.s_bcn[0:32, :], c.ssmA[0:32, sl], c.p[2][0:32, :], ALU.mult)

        @block.sync
        def _(sy, sl=sl):
            sy.wait_ge(cv.sem, cv.n)
            sy.dma_start(out=c.bc_scr[:, sl], in_=c.s_bcn[0:32, :]).then_inc(qs.sem, 16)
        qs.n += 16


def emit_scan(c):
    block = c.block
    qs, qg, qa, cv, cs, ct, cp = c.qs, c.qg, c.qa, c.cv, c.cs, c.ct, c.cp
    BB, CB = 0, 16384
    DTc, DTU, UB, SGB, DBU, HB, CH0, CH1, YG = (0, 1024, 2048, 3072, 4096,
                                                5120, 6144, 7168, 8192)
    DA0, DA1, T2, YACC, Y2 = 0, 1024, 2048, 3072, 4096

    for kch in range(4):
        t0 = kch * TCH
        for n in range(16):
            @block.sync
            def _(sy, kch=kch, n=n, t0=t0):
                if n == 0:
                    sy.wait_ge(cv.sem, cv.n)
                    sy.wait_ge(qs.sem, qs.n)
                sy.dma_start(out=c.ar1[:, BB + n * 1024:BB + (n + 1) * 1024],
                             in_=bass.AP(c.bc_scr, n * T + t0, [[0, 128], [1, 1024]])).then_inc(qs.sem, 16)
                sy.dma_start(out=c.ar1[:, CB + n * 1024:CB + (n + 1) * 1024],
                             in_=bass.AP(c.bc_scr, (16 + n) * T + t0, [[0, 128], [1, 1024]])).then_inc(qs.sem, 16)
            qs.n += 32

        for d in range(NDT):
            if not hasattr(c, "p01_free"):
                c.p01_free = cs.n
                c.t2_free = cv.n
                c.scan_dause = [cv.n, cv.n]

            @block.tensor
            def _(t, kch=kch, d=d, t0=t0, p01=c.p01_free):
                if kch == 0 and d == 0:
                    t.wait_ge(cv.sem, cv.n)
                    t.wait_ge(cs.sem, cs.n)
                else:
                    t.wait_ge(cs.sem, p01)
                t.matmul(c.p[0][:, :], c.s_wdtT[:, d * 128:(d + 1) * 128],
                         c.s_trn[:, t0:t0 + 512], start=True, stop=True)
                t.matmul(c.p[1][:, :], c.s_wdtT[:, d * 128:(d + 1) * 128],
                         c.s_trn[:, t0 + 512:t0 + 1024], start=True, stop=True).then_inc(ct.sem, 1)
            ct.n += 1

            for tq in range(2):
                @block.scalar
                def _(sc, kch=kch, d=d, tq=tq, t2f=c.t2_free):
                    sc = EW(sc, c.cs)
                    sc.wait_ge(ct.sem, ct.n)
                    sc.wait_ge(cv.sem, t2f)
                    sc.activation(c.af[:, T2:T2 + 512], c.p[tq][:, :], AF.Relu,
                                  bias=c.s_cst[:, 16 + d:16 + d + 1])
                    sc.activation(c.af[:, T2 + 512:T2 + 1024], c.p[tq][:, :], AF.Abs,
                                  bias=c.s_cst[:, 16 + d:16 + d + 1])
                    sc.activation(c.af[:, T2 + 512:T2 + 1024], c.af[:, T2 + 512:T2 + 1024], AF.Exp, scale=-1.0)
                    sc.activation(c.af[:, T2 + 512:T2 + 1024], c.af[:, T2 + 512:T2 + 1024], AF.Ln, bias=1.0)
                c.p01_free = cs.n

                @block.vector
                def _(v, kch=kch, d=d, tq=tq):
                    v = EW(v, c.cv)
                    v.wait_ge(cs.sem, cs.n)
                    if d == 0 and tq == 0:
                        v.wait_ge(ct.sem, ct.n)
                    v.tensor_tensor(c.af[:, T2:T2 + 512], c.af[:, T2:T2 + 512],
                                    c.af[:, T2 + 512:T2 + 1024], ALU.add)
                    v.tensor_copy(c.ar2[:, DTc + tq * 512:DTc + (tq + 1) * 512], c.af[:, T2:T2 + 512])
                c.t2_free = cv.n
            c.dtcopy_cv = cv.n

            @block.scalar
            def _(sc, kch=kch, d=d, t0=t0):
                sc.wait_ge(cv.sem, cv.n)
                sc.wait_ge(qa.sem, qa.n)
                sc.dma_start(out=c.ar2[:, UB:UB + 1024], in_=c.u_spill[d, :, t0:t0 + 1024]).then_inc(qa.sem, 16)
                sc.dma_start(out=c.ar2[:, SGB:SGB + 1024], in_=c.sg_spill[d, :, t0:t0 + 1024]).then_inc(qa.sem, 16)
            qa.n += 32

            @block.vector
            def _(v, kch=kch, d=d):
                v = EW(v, c.cv)
                v.wait_ge(qa.sem, qa.n)
                v.wait_ge(cp.sem, cp.n)
                v.tensor_tensor(c.ar2[:, DTU:DTU + 1024], c.ar2[:, DTc:DTc + 1024],
                                c.ar2[:, UB:UB + 1024], ALU.mult)
                v.memset(c.af[:, YACC:YACC + 1024], 0.0)

            c.pool_chuse = getattr(c, "pool_chuse", [cp.n, cp.n])
            c.pool_chuse[0] = c.pool_chuse[1] = cp.n
            for n in range(16):
                dab = DA0 if n % 2 == 0 else DA1
                chb = CH0 if n % 2 == 0 else CH1

                @block.scalar
                def _(sc, kch=kch, d=d, n=n, dab=dab,
                      cvneed=max(c.scan_dause[n % 2] if n >= 2 else c.scan_dause[n % 2],
                                 c.dtcopy_cv)):
                    sc = EW(sc, c.cs)
                    sc.wait_ge(cv.sem, cvneed)
                    if d == 0 and n <= 1:
                        sc.wait_ge(qs.sem, qs.n)
                    sc.activation(c.af[:, dab:dab + 1024], c.ar2[:, DTc:DTc + 1024], AF.Exp,
                                  scale=c.s_alogA[:, d * 16 + n:d * 16 + n + 1])

                @block.vector
                def _(v, kch=kch, d=d, n=n, dab=dab, chb=chb):
                    v = EW(v, c.cv)
                    v.wait_ge(cs.sem, cs.n)
                    v.wait_ge(cp.sem, c.pool_chuse[n % 2])
                    v.tensor_tensor(c.ar2[:, DBU:DBU + 1024], c.ar2[:, DTU:DTU + 1024],
                                    c.ar1[:, BB + n * 1024:BB + (n + 1) * 1024], ALU.mult)
                    if kch == 0:
                        v.tensor_tensor_scan(c.ar2[:, HB:HB + 1024], c.af[:, dab:dab + 1024],
                                             c.ar2[:, DBU:DBU + 1024], 0.0, ALU.mult, ALU.add)
                    else:
                        v.tensor_tensor_scan(c.ar2[:, HB:HB + 1024], c.af[:, dab:dab + 1024],
                                             c.ar2[:, DBU:DBU + 1024],
                                             c.s_hstate[:, d * 16 + n:d * 16 + n + 1],
                                             ALU.mult, ALU.add)
                    v.tensor_copy(c.s_hstate[:, d * 16 + n:d * 16 + n + 1],
                                  c.ar2[:, HB + 1023:HB + 1024])
                    v.tensor_tensor(c.ar2[:, chb:chb + 1024], c.ar2[:, HB:HB + 1024],
                                    c.ar1[:, CB + n * 1024:CB + (n + 1) * 1024], ALU.mult)
                c.scan_dause[n % 2] = cv.n

                @block.gpsimd
                def _(g, kch=kch, d=d, n=n, chb=chb):
                    g = EW(g, c.cp)
                    g.wait_ge(cv.sem, cv.n)
                    g.tensor_tensor(c.af[:, YACC:YACC + 1024], c.af[:, YACC:YACC + 1024],
                                    c.ar2[:, chb:chb + 1024], ALU.add)
                c.pool_chuse[n % 2] = cp.n

            @block.vector
            def _(v, kch=kch, d=d):
                v = EW(v, c.cv)
                v.wait_ge(cp.sem, cp.n)
                v.scalar_tensor_tensor(c.af[:, Y2:Y2 + 1024], c.ar2[:, UB:UB + 1024],
                                       c.s_cst[:, 32 + d:32 + d + 1], c.af[:, YACC:YACC + 1024],
                                       ALU.mult, ALU.add)
                v.tensor_tensor(c.ar2[:, YG + d * 1024:YG + (d + 1) * 1024],
                                c.af[:, Y2:Y2 + 1024], c.ar2[:, SGB:SGB + 1024], ALU.mult)

        emit_wout(c, kch)


def emit_wout(c, kch):
    block = c.block
    qs, qg, cv, cs, ct = c.qs, c.qg, c.cv, c.cs, c.ct
    YG = 8192
    c.woutbuf_use = getattr(c, "woutbuf_use", [ct.n, ct.n])
    c.woutevac_use = getattr(c, "woutevac_use", [cs.n, cs.n])
    c.woutstore_use = getattr(c, "woutstore_use", [qs.n, qs.n])

    for m in range(NKH):
        buf = m % 2
        wb = buf * 2048

        @block.scalar
        def _(sc, kch=kch, m=m, wb=wb, buf=buf):
            sc.wait_ge(ct.sem, c.woutbuf_use[buf])
            if m <= 1:
                sc.wait_ge(cv.sem, cv.n)
                if kch == 0:
                    sc.wait_ge(c.wsem[2], 16)
            sc.dma_start(out=c.ar2[:, wb:wb + 2048].rearrange("p (k q) -> p k q", k=NDT),
                         in_=c.woutbf.ap()[m].rearrange("k p q -> p k q")).then_inc(c.qa.sem, 16)
        c.qa.n += 16

        pbase = 2 + 2 * (m % 2)
        ev = 1024 * (m % 2)

        @block.tensor
        def _(t, kch=kch, m=m, wb=wb, pbase=pbase,
              pfree=None):
            t.wait_ge(c.qa.sem, c.qa.n)
            if m == 0:
                t.wait_ge(cv.sem, cv.n)
            t.wait_ge(cs.sem, c.woutevac_use[m % 2])
            last = None
            for tq in range(2):
                pb = c.p[pbase + tq]
                for k in range(NDT):
                    last = t.matmul(pb[:, :], c.ar2[:, wb + k * 128:wb + (k + 1) * 128],
                                    c.ar2[:, YG + k * 1024 + tq * 512:YG + k * 1024 + (tq + 1) * 512],
                                    start=(k == 0), stop=(k == NDT - 1))
            last.then_inc(ct.sem, 1)
        ct.n += 1
        c.woutbuf_use[buf] = ct.n

        @block.scalar
        def _(sc, kch=kch, m=m, pbase=pbase, ev=ev):
            sc = EW(sc, c.cs)
            sc.wait_ge(ct.sem, ct.n)
            sc.wait_ge(qs.sem, c.woutstore_use[m % 2])
            sc.activation(c.af[:, ev:ev + 512], c.p[pbase][:, :], AF.Copy)
            sc.activation(c.af[:, ev + 512:ev + 1024], c.p[pbase + 1][:, :], AF.Copy)
        c.woutevac_use[m % 2] = cs.n

        @block.sync
        def _(sy, kch=kch, m=m, ev=ev):
            sy.wait_ge(cs.sem, cs.n)
            if True:
                with sy.register(f"wo{kch}_{m}") as role:
                    sy.reg_load(role, c.s_role[0:1, 0:1])
                    with sy.If(make_scalar_value(role) == (kch % 2)):
                        sy.dma_start(out=c.mixer_mine[kch // 2, m, :, :],
                                     in_=c.af[:, ev:ev + 1024]).then_inc(qs.sem, 16)
                    with sy.Else():
                        with sy.register(f"wo2{kch}_{m}") as role2:
                            sy.reg_load(role2, c.s_role[0:1, 0:1])
                            with sy.If(make_scalar_value(role2) == 0):
                                sy.dma_start(out=c.sh_mix[0][kch // 2, m, :, :],
                                             in_=c.af[:, ev:ev + 1024]).then_inc(qs.sem, 16)
                            with sy.Else():
                                sy.dma_start(out=c.sh_mix[1][kch // 2, m, :, :],
                                             in_=c.af[:, ev:ev + 1024]).then_inc(qs.sem, 16)
                qs.n += 16
            sy.wait_ge(qs.sem, qs.n)
        c.woutstore_use[m % 2] = qs.n

    if not c.pairless:
        @block.sync
        def _(sy, kch=kch):
            emit_flag(c, sy, 1 + kch // 2)


def emit_phaseC(c):
    block = c.block
    qs, qg, qa, cv, cs, ct = c.qs, c.qg, c.qa, c.cv, c.cs, c.ct
    PM1, PM2, XC, HID, T1 = 0, 1024, 2048, 3072, 4096

    for s in range(2):
        if not c.pairless:
            @block.sync
            def _(sy, s=s):
                with sy.register(f"c0r{s}") as role:
                    sy.reg_load(role, c.s_role[0:1, 0:1])
                    with sy.If(make_scalar_value(role) == 0):
                        emit_poll(c, sy, c.pollsems[1 + s], c.fl[1], 1 + s, f"m{s}a")
                    with sy.Else():
                        emit_poll(c, sy, c.pollsems[1 + s], c.fl[0], 1 + s, f"m{s}b")

        @block.sync
        def _(sy, s=s):
            sy.wait_ge(cv.sem, cv.n)
            sy.wait_ge(qs.sem, qs.n)
            with sy.register(f"c0b{s}") as role:
                sy.reg_load(role, c.s_role[0:1, 0:1])
                with sy.If(make_scalar_value(role) == 0):
                    sy.dma_start(out=c.ssm1[:, 0:1024],
                                 in_=bass.AP(c.stats_scr, 0 * T + 2 * s * 1024, [[0, 128], [1, 1024]])).then_inc(qs.sem, 16)
                    sy.dma_start(out=c.ssm1[:, 1024:2048],
                                 in_=bass.AP(c.stats_scr, 1 * T + 2 * s * 1024, [[0, 128], [1, 1024]])).then_inc(qs.sem, 16)
                with sy.Else():
                    sy.dma_start(out=c.ssm1[:, 0:1024],
                                 in_=bass.AP(c.stats_scr, 0 * T + (2 * s + 1) * 1024, [[0, 128], [1, 1024]])).then_inc(qs.sem, 16)
                    sy.dma_start(out=c.ssm1[:, 1024:2048],
                                 in_=bass.AP(c.stats_scr, 1 * T + (2 * s + 1) * 1024, [[0, 128], [1, 1024]])).then_inc(qs.sem, 16)
            qs.n += 32
            sy.wait_ge(qs.sem, qs.n)
            sy.nop().then_inc(c.cy.sem, 1)
        c.cy.n += 1

        for j in range(NKH):
            @block.sync
            def _(sy, s=s, j=j):
                sy.wait_ge(c.cy.sem, c.cy.n)
                sy.wait_ge(cv.sem, cv.n)
                sy.wait_ge(qs.sem, qs.n)
                sy.dma_start(out=c.af[:, PM1:PM1 + 1024], in_=c.mixer_mine[s, j, :, :]).then_inc(qs.sem, 16)
                qs.n += 16
                if not c.pairless:
                    with sy.register(f"c1r{s}_{j}") as role:
                        sy.reg_load(role, c.s_role[0:1, 0:1])
                        with sy.If(make_scalar_value(role) == 0):
                            sy.dma_start(out=c.af[:, PM2:PM2 + 1024], in_=c.sh_mix[1][s, j, :, :]).then_inc(qs.sem, 16)
                            sy.dma_start(out=c.af[:, XC:XC + 1024],
                                         in_=c.xT[j * 128:(j + 1) * 128, (2 * s) * 1024:(2 * s + 1) * 1024]).then_inc(qs.sem, 16)
                        with sy.Else():
                            sy.dma_start(out=c.af[:, PM2:PM2 + 1024], in_=c.sh_mix[0][s, j, :, :]).then_inc(qs.sem, 16)
                            sy.dma_start(out=c.af[:, XC:XC + 1024],
                                         in_=c.xT[j * 128:(j + 1) * 128, (2 * s + 1) * 1024:(2 * s + 2) * 1024]).then_inc(qs.sem, 16)
                    qs.n += 32
                else:
                    with sy.register(f"c1r{s}_{j}") as role:
                        sy.reg_load(role, c.s_role[0:1, 0:1])
                        with sy.If(make_scalar_value(role) == 0):
                            sy.dma_start(out=c.af[:, XC:XC + 1024],
                                         in_=c.xT[j * 128:(j + 1) * 128, (2 * s) * 1024:(2 * s + 1) * 1024]).then_inc(qs.sem, 16)
                        with sy.Else():
                            sy.dma_start(out=c.af[:, XC:XC + 1024],
                                         in_=c.xT[j * 128:(j + 1) * 128, (2 * s + 1) * 1024:(2 * s + 2) * 1024]).then_inc(qs.sem, 16)
                    qs.n += 16
                sy.wait_ge(qs.sem, qs.n)
                sy.nop().then_inc(c.cy.sem, 1)
            c.cy.n += 1

            @block.vector
            def _(v, s=s, j=j):
                v = EW(v, c.cv)
                v.wait_ge(c.cy.sem, c.cy.n)
                v.wait_ge(ct.sem, ct.n)
                if c.pairless:
                    v.memset(c.af[:, PM2:PM2 + 1024], 0.0)
                v.tensor_tensor(c.af[:, T1:T1 + 1024], c.af[:, XC:XC + 1024], c.ssm1[:, 0:1024], ALU.mult)
                v.tensor_tensor(c.af[:, T1:T1 + 1024], c.af[:, T1:T1 + 1024], c.ssm1[:, 1024:2048], ALU.add)
                v.tensor_tensor(c.af[:, HID:HID + 1024], c.af[:, T1:T1 + 1024], c.af[:, PM1:PM1 + 1024], ALU.add)
                v.tensor_tensor(c.af[:, HID:HID + 1024], c.af[:, HID:HID + 1024], c.af[:, PM2:PM2 + 1024], ALU.add)
                v.tensor_copy(c.ar2[:, 0:1024], c.af[:, HID:HID + 1024])
                v.tensor_tensor(c.ar2[:, 1024:2048], c.ar2[:, 0:1024], c.ar2[:, 0:1024], ALU.mult)

            @block.sync
            def _(sy, s=s, j=j):
                sy.wait_ge(cv.sem, cv.n)
                sy.dma_start(out=c.hidden_spill[s, j, :, :], in_=c.af[:, HID:HID + 1024]).then_inc(qs.sem, 16)
            qs.n += 16

            @block.tensor
            def _(t, s=s, j=j):
                t.wait_ge(cv.sem, cv.n)
                t.matmul(c.pst[0:1, :], c.s_onesc[:, :], c.ar2[:, 1024:1536],
                         start=(j == 0), stop=(j == NKH - 1))
                t.matmul(c.p[5][0:1, :], c.s_onesc[:, :], c.ar2[:, 1536:2048],
                         start=(j == 0), stop=(j == NKH - 1)).then_inc(ct.sem, 1)
            ct.n += 1

        @block.scalar
        def _(sc, s=s):
            sc = EW(sc, c.cs)
            sc.wait_ge(ct.sem, ct.n)
            sc.activation(c.s_rowf[0:1, 0:512], c.pst[0:1, :], AF.Ln, bias=1e-6, scale=1.0 / H)
            sc.activation(c.s_rowf[0:1, 0:512], c.s_rowf[0:1, 0:512], AF.Exp, scale=-0.5)
            sc.activation(c.s_rowf[0:1, 512:1024], c.p[5][0:1, :], AF.Ln, bias=1e-6, scale=1.0 / H)
            sc.activation(c.s_rowf[0:1, 512:1024], c.s_rowf[0:1, 512:1024], AF.Exp, scale=-0.5)

        @block.vector
        def _(v, s=s):
            v = EW(v, c.cv)
            v.wait_ge(cs.sem, cs.n)
            v.tensor_copy(c.s_rowt[0:1, 0:512], c.s_rowf[0:1, 0:512])
            v.tensor_copy(c.s_rowt[0:1, 512:1024], c.s_rowf[0:1, 512:1024])

        for j in range(NKH):
            @block.sync
            def _(sy, s=s, j=j):
                sy.wait_ge(cv.sem, cv.n)
                sy.wait_ge(qs.sem, qs.n)
                sy.dma_start(out=c.af[:, HID:HID + 1024], in_=c.hidden_spill[s, j, :, :]).then_inc(qs.sem, 16)
                qs.n += 16
                with sy.register(f"c2r{s}_{j}") as role:
                    sy.reg_load(role, c.s_role[0:1, 0:1])
                    with sy.If(make_scalar_value(role) == 0):
                        sy.dma_start(out=c.ar1[:, 16384 + j * 1024:16384 + (j + 1) * 1024],
                                     in_=c.xn_spill[j, :, (2 * s) * 1024:(2 * s + 1) * 1024]).then_inc(qs.sem, 16)
                    with sy.Else():
                        sy.dma_start(out=c.ar1[:, 16384 + j * 1024:16384 + (j + 1) * 1024],
                                     in_=c.xn_spill[j, :, (2 * s + 1) * 1024:(2 * s + 2) * 1024]).then_inc(qs.sem, 16)
                qs.n += 16
                sy.wait_ge(qs.sem, qs.n)
                sy.nop().then_inc(c.cy.sem, 1)
            c.cy.n += 1

            for tq in range(2):
                @block.tensor
                def _(t, s=s, j=j, tq=tq):
                    t.wait_ge(cv.sem, cv.n)
                    t.matmul(c.p[2][:, :], c.s_preln[0:1, j * 128:(j + 1) * 128],
                             c.s_rowt[0:1, tq * 512:(tq + 1) * 512],
                             start=True, stop=True).then_inc(ct.sem, 1)
                ct.n += 1

                @block.vector
                def _(v, s=s, j=j, tq=tq):
                    v = EW(v, c.cv)
                    v.wait_ge(ct.sem, ct.n)
                    v.wait_ge(c.cy.sem, c.cy.n)
                    v.tensor_tensor(c.ar1[:, j * 1024 + tq * 512:j * 1024 + (tq + 1) * 512],
                                    c.af[:, HID + tq * 512:HID + (tq + 1) * 512],
                                    c.p[2][:, :], ALU.mult)

        c.ffnbuf_use = getattr(c, "ffnbuf_use", [ct.n, ct.n])
        c.ffnbuf_use[0] = c.ffnbuf_use[1] = ct.n
        c.ffngu_use = [(cv.sem, cv.n), (cv.sem, cv.n)]
        c.ffnsilu_use = [cv.n, cv.n]
        for mf in range(NMF):
            buf = mf % 2
            wb = buf * 4096

            @block.scalar
            def _(sc, s=s, mf=mf, wb=wb, buf=buf):
                sc.wait_ge(ct.sem, c.ffnbuf_use[buf])
                if s == 0 and mf <= 1:
                    sc.wait_ge(c.wsem[3], 16)
                    sc.wait_ge(c.wsem[4], 16)
                sc.dma_start(out=c.ar2[:, wb:wb + 2048].rearrange("p (k q) -> p k q", k=NKH),
                             in_=c.wgbf.ap()[mf].rearrange("k p q -> p k q")).then_inc(c.qa.sem, 16)
                sc.dma_start(out=c.ar2[:, wb + 2048:wb + 4096].rearrange("p (k q) -> p k q", k=NKH),
                             in_=c.wubf.ap()[mf].rearrange("k p q -> p k q")).then_inc(c.qa.sem, 16)
            c.qa.n += 32

            for tq in range(2):
                pg, pu = c.p[2 * tq], c.p[2 * tq + 1]

                @block.tensor
                def _(t, s=s, mf=mf, wb=wb, tq=tq, pg=pg, pu=pu,
                      gfree=c.ffngu_use[tq]):
                    t.wait_ge(c.qa.sem, c.qa.n)
                    if mf == 0 and tq == 0:
                        t.wait_ge(cv.sem, cv.n)
                        t.wait_ge(cs.sem, cs.n)
                    t.wait_ge(gfree[0], gfree[1])
                    last = None
                    for k in range(NKH):
                        t.matmul(pg[:, :], c.ar2[:, wb + k * 128:wb + (k + 1) * 128],
                                 c.ar1[:, k * 1024 + tq * 512:k * 1024 + (tq + 1) * 512],
                                 start=(k == 0), stop=(k == NKH - 1))
                        last = t.matmul(pu[:, :], c.ar2[:, wb + 2048 + k * 128:wb + 2048 + (k + 1) * 128],
                                        c.ar1[:, k * 1024 + tq * 512:k * 1024 + (tq + 1) * 512],
                                        start=(k == 0), stop=(k == NKH - 1))
                    last.then_inc(ct.sem, 1)
                ct.n += 1

                @block.scalar
                def _(sc, s=s, mf=mf, tq=tq, pg=pg, silufree=c.ffnsilu_use[tq]):
                    sc = EW(sc, c.cs)
                    sc.wait_ge(ct.sem, ct.n)
                    sc.wait_ge(cv.sem, silufree)
                    sc.activation(c.af[:, tq * 512:tq * 512 + 512], pg[:, :], AF.Silu)
                c.ffnsilu_use[tq] = None

                @block.vector
                def _(v, s=s, mf=mf, tq=tq, pu=pu):
                    v = EW(v, c.cv)
                    v.wait_ge(cs.sem, cs.n)
                    v.wait_ge(qs.sem, qs.n)
                    v.tensor_tensor(c.ar2[:, 8192 + tq * 512:8704 + tq * 512],
                                    c.af[:, tq * 512:tq * 512 + 512], pu[:, :], ALU.mult)
                c.ffngu_use[tq] = (cv.sem, cv.n)
                c.ffnsilu_use[tq] = cv.n
            c.ffnbuf_use[buf] = ct.n

            @block.sync
            def _(sy, s=s, mf=mf):
                sy.wait_ge(cv.sem, cv.n)
                sy.dma_start(out=c.v_spill[mf, :, :], in_=c.ar2[:, 8192:9216]).then_inc(qs.sem, 16)
            qs.n += 16

        groups = [(0, 3), (3, 6), (6, 9), (9, 12), (12, 14), (14, 16)]
        for g0, g1 in groups:
            nm = g1 - g0
            @block.scalar
            def _(sc, s=s, g0=g0, g1=g1):
                sc.wait_ge(ct.sem, ct.n)
                sc.wait_ge(cv.sem, cv.n)
                sc.wait_ge(qs.sem, qs.n)
                if s == 0 and g0 == 0:
                    sc.wait_ge(c.wsem[5], 16)
                for mi in range(g0, g1):
                    sc.dma_start(out=c.ar2[:, (mi - g0) * 5632:(mi - g0) * 5632 + 5632].rearrange("p (k q) -> p k q", k=NMF),
                                 in_=c.wdnbf.ap()[mi].rearrange("k p q -> p k q")).then_inc(c.qa.sem, 16)
            c.qa.n += 16 * nm

            ct_base = ct.n
            qv0 = [q.n for q in c.qv]

            @block.scalar
            def _(sc, s=s, g0=g0, g1=g1, ct_base=ct_base):
                sc.wait_ge(cv.sem, cv.n)
                for i in range(4):
                    sc.wait_ge(c.qv[i].sem, c.qv[i].n)
                for k in range(NMF):
                    if k >= 4:
                        sc.wait_ge(ct.sem, ct_base + k - 3)
                    sc.dma_start(out=c.ar2[:, 16896 + (k % 4) * 1024:16896 + (k % 4) * 1024 + 1024],
                                 in_=c.v_spill[k, :, :]).then_inc(c.qv[k % 4].sem, 16)
            for k in range(NMF):
                c.qv[k % 4].n += 16

            @block.tensor
            def _(t, s=s, g0=g0, g1=g1, qv0=qv0):
                t.wait_ge(c.qa.sem, c.qa.n)
                for k in range(NMF):
                    t.wait_ge(c.qv[k % 4].sem, qv0[k % 4] + 16 * (k // 4 + 1))
                    vb = 16896 + (k % 4) * 1024
                    last = None
                    for mi in range(g0, g1):
                        for tq in range(2):
                            last = t.matmul(c.p[2 * (mi - g0) + tq][:, :],
                                            c.ar2[:, (mi - g0) * 5632 + k * 128:(mi - g0) * 5632 + (k + 1) * 128],
                                            c.ar2[:, vb + tq * 512:vb + (tq + 1) * 512],
                                            start=(k == 0), stop=(k == NMF - 1))
                    last.then_inc(ct.sem, 1)
            ct.n += NMF

            for mi in range(g0, g1):
                @block.sync
                def _(sy, s=s, mi=mi, g0=g0):
                    sy.wait_ge(ct.sem, ct.n)
                    sy.wait_ge(cv.sem, cv.n)
                    sy.wait_ge(qs.sem, qs.n)
                    sy.dma_start(out=c.af[:, HID:HID + 1024], in_=c.hidden_spill[s, mi, :, :]).then_inc(qs.sem, 16)
                    qs.n += 16
                    with sy.register(f"c4r{s}_{mi}") as role:
                        sy.reg_load(role, c.s_role[0:1, 0:1])
                        with sy.If(make_scalar_value(role) == 0):
                            sy.dma_start(out=c.af[:, XC:XC + 1024],
                                         in_=c.xT[mi * 128:(mi + 1) * 128, (2 * s) * 1024:(2 * s + 1) * 1024]).then_inc(qs.sem, 16)
                        with sy.Else():
                            sy.dma_start(out=c.af[:, XC:XC + 1024],
                                         in_=c.xT[mi * 128:(mi + 1) * 128, (2 * s + 1) * 1024:(2 * s + 2) * 1024]).then_inc(qs.sem, 16)
                    qs.n += 16
                    sy.wait_ge(qs.sem, qs.n)
                    sy.nop().then_inc(c.cy.sem, 1)
                c.cy.n += 1

                @block.vector
                def _(v, s=s, mi=mi, g0=g0):
                    v = EW(v, c.cv)
                    v.wait_ge(c.cy.sem, c.cy.n)
                    v.wait_ge(ct.sem, ct.n)
                    v.tensor_tensor(c.af[:, T1:T1 + 512], c.p[2 * (mi - g0)][:, :],
                                    c.af[:, HID:HID + 512], ALU.add)
                    v.tensor_tensor(c.af[:, T1 + 512:T1 + 1024], c.p[2 * (mi - g0) + 1][:, :],
                                    c.af[:, HID + 512:HID + 1024], ALU.add)

                @block.gpsimd
                def _(g, s=s, mi=mi):
                    g.wait_ge(cv.sem, cv.n)
                    g.wait_ge(qg.sem, qg.n)
                    g.dma_start(out=c.ar2[:, 20992 + 0:20992 + 2048].rearrange("p (k q) -> p k q", k=NKH),
                                in_=c.qT_cache.ap()[:, mi].rearrange("k p q -> p k q")).then_inc(qg.sem, 16)
                qg.n += 16

                @block.tensor
                def _(t, s=s, mi=mi, g0=g0):
                    t.wait_ge(qg.sem, qg.n)
                    t.wait_ge(cv.sem, cv.n)
                    last = None
                    for tq in range(2):
                        for k in range(NKH):
                            last = t.matmul(c.p[2 * (mi - g0) + tq][:, :],
                                            c.ar2[:, 20992 + k * 128:20992 + (k + 1) * 128],
                                            c.ar1[:, 16384 + k * 1024 + tq * 512:16384 + k * 1024 + (tq + 1) * 512],
                                            start=(k == 0), stop=(k == NKH - 1))
                    last.then_inc(ct.sem, 1)
                ct.n += 1

                @block.scalar
                def _(sc, s=s, mi=mi, g0=g0):
                    sc = EW(sc, c.cs)
                    sc.wait_ge(ct.sem, ct.n)
                    sc.activation(c.af[:, PM1:PM1 + 512], c.p[2 * (mi - g0)][:, :], AF.Silu,
                                  bias=c.s_cst[:, 48 + mi:48 + mi + 1])
                    sc.activation(c.af[:, PM1 + 512:PM1 + 1024], c.p[2 * (mi - g0) + 1][:, :], AF.Silu,
                                  bias=c.s_cst[:, 48 + mi:48 + mi + 1])

                @block.vector
                def _(v, s=s, mi=mi):
                    v = EW(v, c.cv)
                    v.wait_ge(cs.sem, cs.n)
                    v.tensor_tensor(c.af[:, PM2:PM2 + 1024], c.af[:, T1:T1 + 1024],
                                    c.af[:, PM1:PM1 + 1024], ALU.mult)
                    v.tensor_tensor(c.af[:, PM2:PM2 + 1024], c.af[:, PM2:PM2 + 1024],
                                    c.af[:, XC:XC + 1024], ALU.add)

                @block.sync
                def _(sy, s=s, mi=mi):
                    sy.wait_ge(cv.sem, cv.n)
                    sy.dma_start(out=c.out_fm[mi * 128:(mi + 1) * 128, s * 1024:(s + 1) * 1024],
                                 in_=c.af[:, PM2:PM2 + 1024]).then_inc(qs.sem, 16)
                qs.n += 16


def emit_dbg(c):
    block = c.block
    qs = c.qs
    srcs = {
        "u_spill": c.u_spill, "sg_spill": c.sg_spill, "xn_spill": c.xn_spill,
        "stats_scr": c.stats_scr, "bc_scr": c.bc_scr, "qT_cache": c.qT_cache,
        "mixer_mine": c.mixer_mine, "hidden_spill": c.hidden_spill,
        "v_spill": c.v_spill, "g_spill": c.g_spill,
    }
    sbuf_srcs = {"ssm1": c.ssm1, "ssm2": lambda: c.ssmA[0:32, :], "trn": c.s_trn}

    @block.sync
    def _(sy):
        sy.wait_ge(c.cv.sem, c.cv.n)
        sy.wait_ge(c.cs.sem, c.cs.n)
        sy.wait_ge(c.ct.sem, c.ct.n)
        sy.wait_ge(c.cp.sem, c.cp.n)
        sy.wait_ge(c.qa.sem, c.qa.n)
        sy.wait_ge(c.qg.sem, c.qg.n)
        sy.wait_ge(c.qs.sem, c.qs.n)
        for name, t in c.dbg.items():
            key = name[4:] if name.startswith("dbg_") else name
            if key in srcs:
                sy.dma_start(out=t.ap(), in_=srcs[key].ap()).then_inc(qs.sem, 16)
            elif key == "ssm1":
                sy.dma_start(out=t.ap(), in_=c.ssm1[:, :]).then_inc(qs.sem, 16)
            elif key == "ssm2":
                sy.dma_start(out=t.ap(), in_=c.ssmA[0:32, :]).then_inc(qs.sem, 16)
            elif key == "trn":
                sy.dma_start(out=t.ap(), in_=c.s_trn[:, :]).then_inc(qs.sem, 16)
            elif key == "hstate":
                sy.dma_start(out=t.ap(), in_=c.s_hstate[:, :]).then_inc(qs.sem, 16)
            c.qs.n += 16


def emit_tail(c):
    block = c.block

    @block.sync
    def _(sy):
        sy.wait_ge(c.qs.sem, c.qs.n)
        sy.wait_ge(c.qa.sem, c.qa.n)
        sy.wait_ge(c.qg.sem, c.qg.n)
        for q in c.qv:
            sy.wait_ge(q.sem, q.n)


def stage_inputs(inputs, core):
    b, r = core // 2, core % 2
    D = slice(r * DH, (r + 1) * DH)
    f32 = np.float32

    x = np.asarray(inputs["L0_attn_hidden_states"])[b]
    w_in = np.asarray(inputs["L0_attn_w_in"])
    wmy = np.concatenate([w_in[D], w_in[4096 + r * DH:4096 + (r + 1) * DH]], axis=0)
    w_inT = np.ascontiguousarray(
        wmy.T.reshape(NKH, 128, 2 * NDT, 128).transpose(2, 0, 1, 3))
    convw = np.ascontiguousarray(np.asarray(inputs["L0_attn_conv_w"])[D, 0, :].reshape(NDT, 128, 4))
    convb = np.ascontiguousarray(np.asarray(inputs["L0_attn_conv_b"])[D].reshape(NDT, 128).T)
    bdt = np.ascontiguousarray(np.asarray(inputs["L0_attn_b_dt"])[D].reshape(NDT, 128).T)
    dskip = np.ascontiguousarray(np.asarray(inputs["L0_attn_D_skip"])[D].reshape(NDT, 128).T)
    alogm = np.asarray(inputs["L0_attn_A_log"])[D]
    alog = np.ascontiguousarray(alogm.reshape(NDT, 128, N).transpose(1, 0, 2).reshape(128, NDT * N))
    w_xT = np.ascontiguousarray(np.asarray(inputs["L0_attn_w_x"])[:, D].T.reshape(NDT, 128, 160))
    w_dtT = np.ascontiguousarray(np.asarray(inputs["L0_attn_w_dt"])[D].T.reshape(128, NDT, 128).transpose(1, 0, 2))
    dtln_row = np.asarray(inputs["L0_attn_dt_ln_w"]).reshape(1, 128)
    bl = np.asarray(inputs["L0_attn_b_ln_w"])
    cl = np.asarray(inputs["L0_attn_c_ln_w"])
    bc_sel = np.zeros((2, 32), f32)
    bc_sel[0, 0:16] = bl
    bc_sel[1, 16:32] = cl
    grp_sel = np.zeros((32, 2), f32)
    grp_sel[0:16, 0] = 1.0
    grp_sel[16:32, 1] = 1.0
    inln_row = np.asarray(inputs["L0_attn_input_ln_w"]).reshape(1, H)
    preln_row = np.asarray(inputs["L0_attn_pre_ff_ln_w"]).reshape(1, H)
    ident = np.eye(128, dtype=f32)
    w_outT = np.ascontiguousarray(
        np.asarray(inputs["L0_attn_w_out"])[:, D].T.reshape(NDT, 128, NKH, 128).transpose(2, 0, 1, 3))
    w_gT = np.ascontiguousarray(
        np.asarray(inputs["L0_attn_w_gate"]).T.reshape(NKH, 128, NMF, 128).transpose(2, 0, 1, 3))
    w_uT = np.ascontiguousarray(
        np.asarray(inputs["L0_attn_w_up"]).T.reshape(NKH, 128, NMF, 128).transpose(2, 0, 1, 3))
    w_dnT = np.ascontiguousarray(
        np.asarray(inputs["L0_attn_w_down"]).T.reshape(NMF, 128, NKH, 128).transpose(2, 0, 1, 3))
    ffnb = np.ascontiguousarray(np.asarray(inputs["L0_ffn_bias"]).reshape(NKH, 128).T)

    return {
        "xT": np.ascontiguousarray(x.T.astype(f32)),
        "w_inT": w_inT.astype(f32), "convw": convw.astype(f32),
        "convb": convb.astype(f32), "bdt": bdt.astype(f32),
        "dskip": dskip.astype(f32), "alog": alog.astype(f32),
        "w_xT": w_xT.astype(f32), "w_dtT": w_dtT.astype(f32),
        "dtln_row": dtln_row.astype(f32), "bc_sel": bc_sel, "grp_sel": grp_sel,
        "inln_row": inln_row.astype(f32), "preln_row": preln_row.astype(f32),
        "ident": ident,
        "w_outT": w_outT.astype(f32), "w_gT": w_gT.astype(f32),
        "w_uT": w_uT.astype(f32), "w_dnT": w_dnT.astype(f32),
        "w_tern": np.ascontiguousarray(np.asarray(inputs["L0_ffn_weight"]).astype(f32)),
        "ffnb": ffnb.astype(f32),
        "role": np.array([[r]], np.int32),
    }


def kernel(**inputs):
    nc = build_program(stage=STAGE)
    nonce = np.int32((int(time.time_ns() // 1000) & 0x3FFFFFF0) | 8)
    in_maps = []
    for core in range(8):
        m = stage_inputs(inputs, core)
        m["nonce"] = np.array([[nonce]], np.int32)
        in_maps.append(m)
    res = run_bass_kernel_spmd(nc, in_maps, core_ids=list(range(8)))
    out = np.zeros((B_BATCH, T, H), np.float32)
    for core in range(8):
        b, r = core // 2, core % 2
        o = np.asarray(res.results[core]["out_fm"])
        for si in range(2):
            kt = 2 * si + r
            out[b, kt * TCH:(kt + 1) * TCH, :] = o[:, si * TCH:(si + 1) * TCH].T
    return out


B_BATCH = 4
